# version 12 (final)
# speedup vs baseline: 1.4182x; 1.1612x over previous
import math
from contextlib import ExitStack
import numpy as np
import concourse.bass as bass
import concourse.mybir as mybir
from concourse.bass_utils import run_bass_kernel_spmd

F32 = mybir.dt.float32
BF16 = mybir.dt.bfloat16
I32 = mybir.dt.int32
AF = mybir.ActivationFunctionType
ALU = mybir.AluOpType
AX = mybir.AxisListType

D = 2048
HEADS = 16
KVH = 4
DH = 128
T_PRIM = 1024
T_EXT = 1536
NT_P = 8
NT_E = 12
HALO_T = 4
NDC = 16
EPS_QK = 1e-6
EPS_LN = 1e-5
EPS_RMS = float(np.finfo(np.float32).eps)
NEG = -1.0e30
LN_THETA = math.log(10000.0)
INV2PI = 1.0 / (2.0 * math.pi)
TWOPI = 2.0 * math.pi


class Ctr:
    def __init__(self, sem):
        self.sem = sem
        self.v = 0

    def inc(self, n=1):
        self.v += n
        return self.v


class St:
    pass




class _CoalBlock:
    def __init__(self, coal):
        self._c = coal

    def __enter__(self):
        self._c._ensure()
        return self

    def __exit__(self, *a):
        self._c._count += 1
        if self._c._count >= self._c.group:
            self._c.flush()
        return False

    def tensor(self, f):
        self._c._open.tensor(f)

    def vector(self, f):
        self._c._open.vector(f)

    def scalar(self, f):
        self._c._open.scalar(f)

    def gpsimd(self, f):
        self._c._open.gpsimd(f)

    def sync(self, f):
        self._c._open.sync(f)


class _FlushGuard:
    def __init__(self, coal, cm):
        self._coal = coal
        self._cm = cm

    def __enter__(self):
        return self._cm.__enter__()

    def __exit__(self, *a):
        self._coal.flush()
        return self._cm.__exit__(*a)


class CoalNc:
    def __init__(self, nc, tag, group=8):
        self._nc = nc
        self._tag = tag
        self.group = group
        self._count = 0
        self._n = 0
        self._open = None
        self._cm = None

    def _ensure(self):
        if self._open is None:
            self._n += 1
            self._cm = self._nc.Block(f"mb_{self._tag}_{self._n}")
            self._open = self._cm.__enter__()

    def flush(self):
        if self._open is not None:
            self._cm.__exit__(None, None, None)
            self._open = None
            self._cm = None
        self._count = 0

    def Block(self, name=None, **k):
        return _CoalBlock(self)

    def sbuf_tensor(self, name, *a, **k):
        return _FlushGuard(self, self._nc.sbuf_tensor(f"{name}_{self._tag}", *a, **k))

    def psum_tensor(self, name, *a, **k):
        return _FlushGuard(self, self._nc.psum_tensor(f"{name}_{self._tag}", *a, **k))

    def semaphore(self, name=None, **k):
        return _FlushGuard(self, self._nc.semaphore(f"{name}_{self._tag}" if name else None, **k))

    def __getattr__(self, nm):
        return getattr(self._nc, nm)


class NcU:
    def __init__(self, nc, tag):
        self._nc = nc
        self._tag = tag

    def sbuf_tensor(self, name, *a, **k):
        return self._nc.sbuf_tensor(f"{name}_{self._tag}", *a, **k)

    def psum_tensor(self, name, *a, **k):
        return self._nc.psum_tensor(f"{name}_{self._tag}", *a, **k)

    def semaphore(self, name=None, **k):
        return self._nc.semaphore(f"{name}_{self._tag}" if name else None, **k)

    def Block(self, name=None, **k):
        return self._nc.Block(f"{name}_{self._tag}" if name else None, **k)

    def __getattr__(self, nm):
        return getattr(self._nc, nm)




def emit_consts(nc, ctx, st, io):
    st.ones_f = ctx.enter_context(nc.sbuf_tensor("ones_f", [1, 128], F32))
    st.ident = ctx.enter_context(nc.sbuf_tensor("ident", [128, 128], BF16))
    st.iota_p = ctx.enter_context(nc.sbuf_tensor("iota_p", [128, 1], F32))
    st.invf_b = ctx.enter_context(nc.sbuf_tensor("invf_b", [128, 64], F32))
    st.qg = ctx.enter_context(nc.sbuf_tensor("qg", [128, 128], F32))
    st.kg = ctx.enter_context(nc.sbuf_tensor("kg", [128, 128], F32))
    st.mask = ctx.enter_context(nc.sbuf_tensor("mask", [128, 640], F32))
    st.killb = ctx.enter_context(nc.sbuf_tensor("killb", [128, 1], F32))
    st.pos0b = ctx.enter_context(nc.sbuf_tensor("pos0b", [128, 1], F32))
    st.scr11 = ctx.enter_context(nc.sbuf_tensor("scr11", [1, 2], F32))
    st.row128 = ctx.enter_context(nc.sbuf_tensor("row128", [1, 256], F32))
    st.cpool = ctx.enter_context(nc.sbuf_tensor("cpool", [128, 8], F32))
    st.ident_f = ctx.enter_context(nc.sbuf_tensor("ident_f", [128, 128], F32))
    st.sem_ld = ctx.enter_context(nc.semaphore("g_ld"))
    st.sem_pe = ctx.enter_context(nc.semaphore("g_pe"))
    st.sem_ve = ctx.enter_context(nc.semaphore("g_ve"))
    st.sem_ac = ctx.enter_context(nc.semaphore("g_ac"))
    st.sem_st = ctx.enter_context(nc.semaphore("g_st"))
    st.sem_vs = ctx.enter_context(nc.semaphore("g_vs"))
    st.sem_ss = ctx.enter_context(nc.semaphore("g_ss"))
    st.sem_gs = ctx.enter_context(nc.semaphore("g_gs"))
    st.sem_la = ctx.enter_context(nc.semaphore("g_la"))
    st.sem_lb = ctx.enter_context(nc.semaphore("g_lb"))
    st.sem_sa = ctx.enter_context(nc.semaphore("g_sa"))
    st.sem_sb = ctx.enter_context(nc.semaphore("g_sb"))
    st.ld = Ctr(st.sem_ld); st.pe = Ctr(st.sem_pe); st.ve = Ctr(st.sem_ve)
    st.ac = Ctr(st.sem_ac); st.stc = Ctr(st.sem_st)
    st.vs = Ctr(st.sem_vs); st.ss_c = Ctr(st.sem_ss); st.gs = Ctr(st.sem_gs)
    st.lr = [Ctr(st.sem_la), Ctr(st.sem_lb)]
    st.sr = [Ctr(st.sem_sa), Ctr(st.sem_sb)]
    nc.const_aps.register(F32, math.pi / 2.0, st.cpool[:, 0:1])
    nc.const_aps.register(F32, EPS_QK, st.cpool[:, 1:2])
    nc.const_aps.register(F32, EPS_LN, st.cpool[:, 2:3])
    nc.const_aps.register(F32, EPS_RMS, st.cpool[:, 3:4])

    with nc.psum_tensor("p_c", [128, 128], F32) as p_c:
        with nc.Block("c1") as block:
            @block.sync
            def _(sync):
                sync.dma_start(out=st.scr11[:, 0:1], in_=io["pos0"]).then_inc(st.sem_ld, 16)
                sync.dma_start(out=st.scr11[:, 1:2], in_=io["kill"]).then_inc(st.sem_ld, 16)
                sync.dma_start(out=st.row128[:, 0:128], in_=io["qn_w"].unsqueeze(0)).then_inc(st.sem_ld, 16)
                sync.dma_start(out=st.row128[:, 128:256], in_=io["kn_w"].unsqueeze(0)).then_inc(st.sem_ld, 16)
            @block.gpsimd
            def _(g):
                g.memset(st.ones_f[:], 1.0)
                g.memset(st.ident[:], 1.0)
                g.memset(st.mask[:], 0.0)
                g.iota(st.iota_p[:], [[1, 1]], channel_multiplier=1,
                       allow_small_or_imprecise_dtypes=True)
                g.iota(st.invf_b[0:1, :], [[1, 64]], channel_multiplier=0,
                       allow_small_or_imprecise_dtypes=True)
                g.memset(st.cpool[:, 0:1], math.pi / 2.0)
                g.memset(st.cpool[:, 1:2], EPS_QK)
                g.memset(st.cpool[:, 2:3], EPS_LN)
                g.memset(st.cpool[:, 3:4], EPS_RMS)
                g.memset(st.ident_f[:], 1.0)
        st.ld.inc(64)
        with nc.Block("c2") as block:
            @block.gpsimd
            def _(g):
                g.affine_select(out=st.ident[:], in_=st.ident[:], pattern=[[-1, 128]],
                                compare_op=ALU.is_equal, fill=0.0, channel_multiplier=1)
                g.affine_select(out=st.ident_f[:], in_=st.ident_f[:], pattern=[[-1, 128]],
                                compare_op=ALU.is_equal, fill=0.0, channel_multiplier=1)
                g.affine_select(out=st.mask[:, 0:128], in_=st.mask[:, 0:128], pattern=[[1, 128]],
                                compare_op=ALU.is_ge, fill=NEG, channel_multiplier=-1)
                g.affine_select(out=st.mask[:, 512:640], in_=st.mask[:, 512:640], pattern=[[-1, 128]],
                                compare_op=ALU.is_ge, fill=NEG, channel_multiplier=1)
            @block.scalar
            def _(s):
                s.wait_ge(st.sem_ld, st.ld.v)
                s.activation(out=st.invf_b[0:1, :], in_=st.invf_b[0:1, :], func=AF.Exp,
                             scale=-LN_THETA / 64.0)
            @block.vector
            def _(v):
                v.wait_ge(st.sem_ld, st.ld.v)
                sc = DH ** -0.5
                v.tensor_scalar(out=st.row128[:, 0:128], in0=st.row128[:, 0:128],
                                scalar1=sc, scalar2=sc, op0=ALU.mult, op1=ALU.add)
                v.tensor_scalar(out=st.row128[:, 128:256], in0=st.row128[:, 128:256],
                                scalar1=1.0, scalar2=1.0, op0=ALU.mult, op1=ALU.add)
        with nc.Block("c3") as block:
            @block.tensor
            def _(t):
                t.matmul(p_c[:, 0:1], st.ones_f[:], st.scr11[:, 0:1], start=True, stop=True)
                t.matmul(p_c[:, 1:2], st.ones_f[:], st.scr11[:, 1:2], start=True, stop=True)
                t.matmul(p_c[:, 64:128], st.ones_f[:], st.invf_b[0:1, :],
                         start=True, stop=True).then_inc(st.sem_pe, 1)
        st.pe.inc()
        with nc.Block("c4") as block:
            pe_c4 = st.pe.v
            @block.vector
            def _(v):
                v.wait_ge(st.sem_pe, pe_c4)
                v.tensor_copy(out=st.pos0b[:], in_=p_c[:, 0:1])
                v.tensor_copy(out=st.killb[:], in_=p_c[:, 1:2])
                v.tensor_copy(out=st.invf_b[:], in_=p_c[:, 64:128])
        with nc.Block("c5") as block:
            @block.tensor
            def _(t):
                t.matmul(p_c[:], st.ones_f[:], st.row128[:, 0:128],
                         start=True, stop=True).then_inc(st.sem_pe, 1)
        st.pe.inc()
        with nc.Block("c6") as block:
            pe_c6 = st.pe.v
            @block.vector
            def _(v):
                v.wait_ge(st.sem_pe, pe_c6)
                v.tensor_copy(out=st.qg[:], in_=p_c[:])
        with nc.Block("c7") as block:
            @block.tensor
            def _(t):
                t.matmul(p_c[:], st.ones_f[:], st.row128[:, 128:256],
                         start=True, stop=True).then_inc(st.sem_pe, 1)
        st.pe.inc()
        with nc.Block("c8") as block:
            pe_c8 = st.pe.v
            @block.vector
            def _(v):
                v.wait_ge(st.sem_pe, pe_c8)
                v.tensor_copy(out=st.kg[:], in_=p_c[:])


def emit_load_w_bf16(nc, st, name, jobs, stage, pre_wait=None):
    w = jobs[0][0].shape[1]
    base_ve = st.ve.v
    base_lr = [st.lr[0].v, st.lr[1].v]
    with nc.Block(name) as block:
        @block.sync
        def _(sync):
            for n, (src, dst) in enumerate(jobs):
                if n < 2:
                    sync.wait_ge(st.sem_ve, base_ve)
                else:
                    sync.wait_ge(st.sem_ve, base_ve + n - 1)
                sync.dma_start(out=stage[:, (n % 2) * w:(n % 2 + 1) * w],
                               in_=src).then_inc(st.lr[n % 2].sem, 16)
        @block.vector
        def _(v):
            for sem, val in (pre_wait or []):
                v.wait_ge(sem, val)
            for n, (src, dst) in enumerate(jobs):
                v.wait_ge(st.lr[n % 2].sem, base_lr[n % 2] + (n // 2 + 1) * 16)
                v.tensor_copy(out=dst, in_=stage[:, (n % 2) * w:(n % 2 + 1) * w]).then_inc(st.sem_ve, 1)
    nj = len(jobs)
    st.lr[0].inc(16 * ((nj + 1) // 2)); st.lr[1].inc(16 * (nj // 2))
    st.ve.inc(nj)


def emit_rope_tables(nc, st, name, tile, buf):
    base_ve, base_ac = st.ve.v, st.ac.v
    with nc.Block(name) as block:
        @block.vector
        def _(v):
            def vsync(inst):
                inst.then_inc(st.sem_vs, 1)
                v.wait_ge(st.sem_vs, st.vs.inc())
            vsync(v.tensor_scalar(out=buf.rp[:, 128:129], in0=st.iota_p[:], scalar1=st.pos0b[:],
                                  scalar2=float(128 * tile), op0=ALU.add, op1=ALU.add))
            vsync(v.tensor_scalar(out=buf.rp[:, 0:64], in0=st.invf_b[:], scalar1=buf.rp[:, 128:129],
                                  scalar2=None, op0=ALU.mult))
            vsync(v.tensor_scalar(out=buf.rp[:, 64:128], in0=buf.rp[:, 0:64], scalar1=INV2PI,
                                  scalar2=None, op0=ALU.mult))
            vsync(v.tensor_copy(out=buf.ri[:], in_=buf.rp[:, 64:128]))
            vsync(v.tensor_copy(out=buf.rp[:, 64:128], in_=buf.ri[:]))
            v.scalar_tensor_tensor(out=buf.rp[:, 64:128], in0=buf.rp[:, 64:128], scalar=-TWOPI,
                                   in1=buf.rp[:, 0:64], op0=ALU.mult, op1=ALU.add).then_inc(st.sem_ve, 1)
            vsync(v.tensor_scalar(out=buf.rp[:, 129:193], in0=buf.rp[:, 0:64], scalar1=INV2PI,
                                  scalar2=0.25, op0=ALU.mult, op1=ALU.add))
            vsync(v.tensor_copy(out=buf.ri[:], in_=buf.rp[:, 129:193]))
            vsync(v.tensor_copy(out=buf.rp[:, 129:193], in_=buf.ri[:]))
            v.scalar_tensor_tensor(out=buf.rp[:, 129:193], in0=buf.rp[:, 129:193], scalar=-TWOPI,
                                   in1=buf.rp[:, 0:64], op0=ALU.mult, op1=ALU.add).then_inc(st.sem_ve, 1)
            v.wait_ge(st.sem_ac, base_ac + 1)
            v.tensor_scalar(out=buf.cs[:, 128:192], in0=buf.cs[:, 192:256], scalar1=-1.0,
                            scalar2=None, op0=ALU.mult)
            v.wait_ge(st.sem_ac, base_ac + 2)
            v.tensor_copy(out=buf.cs[:, 64:128], in_=buf.cs[:, 0:64])
        @block.scalar
        def _(s):
            s.wait_ge(st.sem_ve, base_ve + 1)
            s.activation(out=buf.cs[:, 192:256], in_=buf.rp[:, 64:128],
                         func=AF.Sin).then_inc(st.sem_ac, 1)
            s.wait_ge(st.sem_ve, base_ve + 2)
            s.activation(out=buf.cs[:, 0:64], in_=buf.rp[:, 129:193], func=AF.Sin,
                         bias=math.pi / 2.0).then_inc(st.sem_ac, 1)
    st.ve.inc(2); st.ac.inc(2)


def emit_norm_rope_heads(nc, st, name, src_f, nheads, gain, dst_bf, buf):
    base_ve, base_ac = st.ve.v, st.ac.v
    with nc.Block(name) as block:
        @block.vector
        def _(v):
            def vsync(inst):
                inst.then_inc(st.sem_vs, 1)
                v.wait_ge(st.sem_vs, st.vs.inc())
            for h in range(nheads):
                vsync(v.tensor_mul(buf.hs[:, 0:128], src_f[:, h * 128:(h + 1) * 128],
                                   src_f[:, h * 128:(h + 1) * 128]))
                ins = v.reduce_sum(out=buf.ss[:, h:h + 1], in_=buf.hs[:, 0:128], axis=AX.X)
                if h == nheads - 1:
                    ins.then_inc(st.sem_ve, 1)
            v.wait_ge(st.sem_ac, base_ac + 1)
            for h in range(nheads):
                vsync(v.scalar_tensor_tensor(out=buf.hs[:, 0:128],
                                             in0=src_f[:, h * 128:(h + 1) * 128],
                                             scalar=buf.ss[:, nheads + h:nheads + h + 1],
                                             in1=gain[:], op0=ALU.mult, op1=ALU.mult))
                v.tensor_mul(buf.hs[:, 128:256], buf.hs[:, 0:128], buf.cs[:, 0:128])
                v.tensor_mul(buf.hs[:, 256:320], buf.hs[:, 64:128], buf.cs[:, 128:192])
                vsync(v.tensor_mul(buf.hs[:, 320:384], buf.hs[:, 0:64], buf.cs[:, 192:256]))
                v.tensor_add(dst_bf[:, h * 128:(h + 1) * 128], buf.hs[:, 128:256],
                             buf.hs[:, 256:384])
        @block.scalar
        def _(s):
            s.wait_ge(st.sem_ve, base_ve + 1)
            s.activation(out=buf.ss[:, nheads:2 * nheads], in_=buf.ss[:, 0:nheads],
                         func=AF.Ln, scale=1.0 / DH, bias=EPS_QK).then_inc(st.sem_ss, 1)
            s.wait_ge(st.sem_ss, st.ss_c.inc())
            s.activation(out=buf.ss[:, nheads:2 * nheads], in_=buf.ss[:, nheads:2 * nheads],
                         func=AF.Exp, scale=-0.5).then_inc(st.sem_ac, 1)
    st.ve.inc(1); st.ac.inc(1)


def emit_transpose_set(nc, st, name, pairs, p_a, p_b, pre_wait=None):
    base_pe, base_ve = st.pe.v, st.ve.v
    with nc.Block(name) as block:
        @block.tensor
        def _(te):
            for sem, val in (pre_wait or []):
                te.wait_ge(sem, val)
            for i, (src, dst) in enumerate(pairs):
                if i == 0:
                    te.wait_ge(st.sem_ve, base_ve)
                if i >= 2:
                    te.wait_ge(st.sem_ve, base_ve + i - 1)
                te.transpose(p_a[:] if i % 2 == 0 else p_b[:], src,
                             st.ident[:]).then_inc(st.sem_pe, 1)
        @block.vector
        def _(v):
            for i, (src, dst) in enumerate(pairs):
                v.wait_ge(st.sem_pe, base_pe + i + 1)
                v.tensor_copy(out=dst, in_=p_a[:] if i % 2 == 0 else p_b[:]).then_inc(st.sem_ve, 1)
    st.pe.inc(len(pairs)); st.ve.inc(len(pairs))


def emit_ln(nc, st, name, row_f32, lnst, scratch, out_dram=None, out_sb=None,
            pre_wait=None):
    base_ve, base_ac = st.ve.v, st.ac.v
    with nc.Block(name) as block:
        @block.scalar
        def _(s):
            for sem, val in (pre_wait or []):
                s.wait_ge(sem, val)
            s.activation(out=scratch, in_=row_f32, func=AF.Square).then_inc(st.sem_ac, 1)
            s.wait_ge(st.sem_ve, base_ve + 1)
            s.activation(out=lnst[:, 6:7], in_=lnst[:, 5:6], func=AF.Sqrt,
                         bias=EPS_LN).then_inc(st.sem_ac, 1)
        @block.vector
        def _(v):
            for sem, val in (pre_wait or []):
                v.wait_ge(sem, val)
            def vsync(inst):
                inst.then_inc(st.sem_vs, 1)
                v.wait_ge(st.sem_vs, st.vs.inc())
            vsync(v.reduce_sum(out=lnst[:, 0:1], in_=row_f32, axis=AX.X))
            vsync(v.tensor_scalar(out=lnst[:, 1:2], in0=lnst[:, 0:1], scalar1=1.0 / D,
                                  scalar2=None, op0=ALU.mult))
            v.wait_ge(st.sem_ac, base_ac + 1)
            vsync(v.reduce_sum(out=lnst[:, 2:3], in_=scratch, axis=AX.X))
            vsync(v.tensor_scalar(out=lnst[:, 3:4], in0=lnst[:, 2:3], scalar1=1.0 / D,
                                  scalar2=None, op0=ALU.mult))
            vsync(v.tensor_mul(lnst[:, 4:5], lnst[:, 1:2], lnst[:, 1:2]))
            v.tensor_sub(lnst[:, 5:6], lnst[:, 3:4], lnst[:, 4:5]).then_inc(st.sem_ve, 1)
            v.wait_ge(st.sem_ac, base_ac + 2)
            vsync(v.reciprocal(out=lnst[:, 7:8], in_=lnst[:, 6:7]))
            v.tensor_scalar(out=row_f32, in0=row_f32, scalar1=lnst[:, 1:2],
                            scalar2=lnst[:, 7:8], op0=ALU.subtract,
                            op1=ALU.mult).then_inc(st.sem_vs, 1)
            st.vs.inc()
            if out_sb is not None:
                v.wait_ge(st.sem_vs, st.vs.v)
                v.tensor_copy(out=out_sb, in_=row_f32).then_inc(st.sem_vs, 1)
                st.vs.inc()
    st.ve.inc(1); st.ac.inc(2)
    if out_dram is not None:
        final_vs = st.vs.v
        with nc.Block(name + "_st") as block:
            @block.sync
            def _(sync):
                sync.wait_ge(st.sem_vs, final_vs)
                sync.dma_start(out=out_dram, in_=row_f32).then_inc(st.sem_st, 16)
        st.stc.inc(16)


def _dump(nc, st, name, sb_slice, dram_ap):
    with nc.Block(name) as block:
        @block.sync
        def _(sync):
            sync.dma_start(out=dram_ap, in_=sb_slice).then_inc(st.sem_st, 16)
    st.stc.inc(16)



def emit_l0_attn(nc, ctx, st, io):
    xh_tok, xh_fmT = io["xh_tok"], io["xh_fmT"]
    dbg = io.get("dbg")

    octx = ExitStack()
    ctx_fm = octx.enter_context(nc.sbuf_tensor("ctx_fm", [128, NT_P * D], BF16))

    with ExitStack() as f1ctx:
        q_fm = f1ctx.enter_context(nc.sbuf_tensor("q_fm", [128, HEADS * T_PRIM], BF16))
        k_fm = f1ctx.enter_context(nc.sbuf_tensor("k_fm", [128, KVH * T_EXT], BF16))
        v_tok = f1ctx.enter_context(nc.sbuf_tensor("v_tok", [128, NT_E * 512], BF16))
        xh_fm = f1ctx.enter_context(nc.sbuf_tensor("xh_fm", [128, NDC * T_EXT], BF16))
        wstage = f1ctx.enter_context(nc.sbuf_tensor("wstage", [128, 2 * T_EXT], F32))

        emit_load_w_bf16(nc, st, "f1_ldx",
                         [(xh_fmT[c * 128:(c + 1) * 128, :],
                           xh_fm[:, c * T_EXT:(c + 1) * T_EXT]) for c in range(NDC)],
                         wstage)

        class Buf:
            pass
        buf = Buf()
        buf.rp = f1ctx.enter_context(nc.sbuf_tensor("rp", [128, 224], F32))
        buf.ri = f1ctx.enter_context(nc.sbuf_tensor("ri", [128, 64], I32))
        buf.cs = f1ctx.enter_context(nc.sbuf_tensor("cs", [128, 256], F32))
        buf.ss = f1ctx.enter_context(nc.sbuf_tensor("ss", [128, 32], F32))
        buf.hs = f1ctx.enter_context(nc.sbuf_tensor("hs", [128, 384], F32))
        prj_f = f1ctx.enter_context(nc.sbuf_tensor("prj_f", [128, 1024], F32))
        nr_bf = f1ctx.enter_context(nc.sbuf_tensor("nr_bf", [128, 1024], BF16))

        with (
            nc.sbuf_tensor("wk_bf", [128, NDC * 512], BF16) as wk_bf,
            nc.sbuf_tensor("wv_bf", [128, NDC * 512], BF16) as wv_bf,
            nc.psum_tensor("p_k", [128, 512], F32) as p_k,
            nc.psum_tensor("p_v", [128, 512], F32) as p_v,
            nc.psum_tensor("p_ta", [128, 128], BF16) as p_ta,
            nc.psum_tensor("p_tb", [128, 128], BF16) as p_tb,
        ):
            emit_load_w_bf16(nc, st, "f1_ldwk",
                             [(io["wkT"][c * 128:(c + 1) * 128, :],
                               wk_bf[:, c * 512:(c + 1) * 512]) for c in range(NDC)],
                             wstage)
            emit_load_w_bf16(nc, st, "f1_ldwv",
                             [(io["wvT"][c * 128:(c + 1) * 128, :],
                               wv_bf[:, c * 512:(c + 1) * 512]) for c in range(NDC)],
                             wstage)
            for t in range(NT_E):
                ve_war_kv = st.ve.v
                with nc.Block(f"f1_kv_mm_{t}") as block:
                    @block.tensor
                    def _(te):
                        te.wait_ge(st.sem_ve, ve_war_kv)
                        for c in range(NDC):
                            xs = xh_fm[:, c * T_EXT + t * 128:c * T_EXT + (t + 1) * 128]
                            te.matmul(p_k[:], xs, wk_bf[:, c * 512:(c + 1) * 512],
                                      start=(c == 0), stop=(c == NDC - 1))
                            ins = te.matmul(p_v[:], xs, wv_bf[:, c * 512:(c + 1) * 512],
                                            start=(c == 0), stop=(c == NDC - 1))
                            if c == NDC - 1:
                                ins.then_inc(st.sem_pe, 1)
                st.pe.inc()
                pe_kv = st.pe.v
                with nc.Block(f"f1_kv_ev_{t}") as block:
                    @block.vector
                    def _(v):
                        v.wait_ge(st.sem_pe, pe_kv)
                        v.tensor_copy(out=prj_f[:, 0:512], in_=p_k[:])
                        v.tensor_copy(out=v_tok[:, t * 512:(t + 1) * 512],
                                      in_=p_v[:]).then_inc(st.sem_ve, 1)
                st.ve.inc()
                emit_rope_tables(nc, st, f"f1_kv_rt_{t}", t, buf)
                emit_norm_rope_heads(nc, st, f"f1_kv_nr_{t}", prj_f[:, 0:512], KVH,
                                     st.kg, nr_bf, buf)
                emit_transpose_set(nc, st, f"f1_kv_tr_{t}",
                                   [(nr_bf[:, h * 128:(h + 1) * 128],
                                     k_fm[:, h * T_EXT + t * 128:h * T_EXT + (t + 1) * 128])
                                    for h in range(KVH)], p_ta, p_tb)

        with (
            nc.sbuf_tensor("wq_bf", [128, 2 * NDC * 512], BF16) as wq_bf,
            nc.psum_tensor("p_q0", [128, 512], F32) as p_q0,
            nc.psum_tensor("p_q1", [128, 512], F32) as p_q1,
            nc.psum_tensor("p_tc", [128, 128], BF16) as p_tc,
            nc.psum_tensor("p_td", [128, 128], BF16) as p_td,
        ):
            for qpass in range(2):
                jobs = []
                for fp in range(2):
                    f0 = (qpass * 2 + fp) * 512
                    jobs += [(io["wqT"][c * 128:(c + 1) * 128, f0:f0 + 512],
                              wq_bf[:, fp * NDC * 512 + c * 512:fp * NDC * 512 + (c + 1) * 512])
                             for c in range(NDC)]
                emit_load_w_bf16(nc, st, f"f1_ldwq_{qpass}", jobs, wstage)
                for tp in range(NT_P):
                    t = tp + HALO_T
                    ve_war_q = st.ve.v
                    with nc.Block(f"f1_q_mm_{qpass}_{tp}") as block:
                        @block.tensor
                        def _(te):
                            te.wait_ge(st.sem_ve, ve_war_q)
                            for c in range(NDC):
                                xs = xh_fm[:, c * T_EXT + t * 128:c * T_EXT + (t + 1) * 128]
                                te.matmul(p_q0[:], xs, wq_bf[:, c * 512:(c + 1) * 512],
                                          start=(c == 0), stop=(c == NDC - 1))
                                ins = te.matmul(p_q1[:], xs,
                                                wq_bf[:, NDC * 512 + c * 512:NDC * 512 + (c + 1) * 512],
                                                start=(c == 0), stop=(c == NDC - 1))
                                if c == NDC - 1:
                                    ins.then_inc(st.sem_pe, 1)
                    st.pe.inc()
                    pe_q = st.pe.v
                    with nc.Block(f"f1_q_ev_{qpass}_{tp}") as block:
                        @block.vector
                        def _(v):
                            v.wait_ge(st.sem_pe, pe_q)
                            v.tensor_copy(out=prj_f[:, 0:512], in_=p_q0[:])
                            v.tensor_copy(out=prj_f[:, 512:1024],
                                          in_=p_q1[:]).then_inc(st.sem_ve, 1)
                    st.ve.inc()
                    emit_rope_tables(nc, st, f"f1_q_rt_{qpass}_{tp}", t, buf)
                    emit_norm_rope_heads(nc, st, f"f1_q_nr_{qpass}_{tp}", prj_f, 8,
                                         st.qg, nr_bf, buf)
                    emit_transpose_set(nc, st, f"f1_q_tr_{qpass}_{tp}",
                                       [(nr_bf[:, i * 128:(i + 1) * 128],
                                         q_fm[:, (qpass * 8 + i) * T_PRIM + tp * 128:
                                              (qpass * 8 + i) * T_PRIM + (tp + 1) * 128])
                                        for i in range(8)], p_tc, p_td)

        if dbg:
            _dump(nc, st, "dbg_qfm", q_fm[:], dbg["q_fm"])
            _dump(nc, st, "dbg_kfm", k_fm[:], dbg["k_fm"])
            _dump(nc, st, "dbg_vtok", v_tok[:], dbg["v_tok"])
            _dump(nc, st, "dbg_xhfm", xh_fm[:], dbg["xh_fm"])

        with (
            nc.sbuf_tensor("s_sb", [128, 640], F32) as s_sb,
            nc.sbuf_tensor("e_bf", [128, 640], BF16) as e_bf,
            nc.sbuf_tensor("eT", [128, 640], BF16) as eT,
            nc.sbuf_tensor("mxs", [128, 4], F32) as mxs,
            nc.sbuf_tensor("ctx_tok", [128, D], BF16) as ctx_tok,
            nc.psum_tensor("p_s", [128, 640], F32) as p_s,
            nc.psum_tensor("p_te", [128, 128], BF16) as p_te,
            nc.psum_tensor("p_tf", [128, 128], BF16) as p_tf,
            nc.psum_tensor("p_cx", [128, 128], F32) as p_cx,
        ):
            for tp in range(NT_P):
                for h in range(HEADS):
                    hk = h // (HEADS // KVH)
                    qsl = q_fm[:, h * T_PRIM + tp * 128:h * T_PRIM + (tp + 1) * 128]
                    P, V, A = st.pe.v, st.ve.v, st.ac.v
                    vs_war, ac_war2 = st.vs.v, st.ac.v
                    with nc.Block(f"f1_a_{tp}_{h}") as block:
                        @block.tensor
                        def _(te):
                            te.wait_ge(st.sem_vs, vs_war)
                            te.wait_ge(st.sem_ac, ac_war2)
                            te.matmul(p_s[:, 0:512], qsl,
                                      k_fm[:, hk * T_EXT + tp * 128:hk * T_EXT + tp * 128 + 512],
                                      start=True, stop=True)
                            te.matmul(p_s[:, 512:640], qsl,
                                      k_fm[:, hk * T_EXT + tp * 128 + 512:hk * T_EXT + tp * 128 + 640],
                                      start=True, stop=True).then_inc(st.sem_pe, 1)
                            te.wait_ge(st.sem_ac, A + 1)
                            for r in range(5):
                                if r >= 2:
                                    te.wait_ge(st.sem_ac, A + r)
                                te.transpose(p_te[:] if r % 2 == 0 else p_tf[:],
                                             e_bf[:, r * 128:(r + 1) * 128],
                                             st.ident[:]).then_inc(st.sem_pe, 1)
                            for r in range(5):
                                te.wait_ge(st.sem_ac, A + 2 + r)
                                ins = te.matmul(p_cx[:], eT[:, r * 128:(r + 1) * 128],
                                                v_tok[:, (tp + r) * 512 + hk * 128:(tp + r) * 512 + (hk + 1) * 128],
                                                start=(r == 0), stop=(r == 4))
                                if r == 4:
                                    ins.then_inc(st.sem_pe, 1)
                        @block.vector
                        def _(v):
                            def vsync(inst):
                                inst.then_inc(st.sem_vs, 1)
                                v.wait_ge(st.sem_vs, st.vs.inc())
                            v.wait_ge(st.sem_pe, P + 1)
                            vsync(v.tensor_add(s_sb[:], p_s[:], st.mask[:]))
                            if tp < HALO_T:
                                ncol = (HALO_T - tp) * 128
                                vsync(v.tensor_scalar(out=s_sb[:, 0:ncol], in0=s_sb[:, 0:ncol],
                                                      scalar1=st.killb[:], scalar2=None, op0=ALU.add))
                            v.reduce_max(out=mxs[:, 0:1], in_=s_sb[:], axis=AX.X,
                                         negate=True).then_inc(st.sem_ve, 1)
                            v.wait_ge(st.sem_ac, A + 1)
                            vsync(v.reduce_sum(out=mxs[:, 1:2], in_=e_bf[:], axis=AX.X))
                            v.reciprocal(out=mxs[:, 2:3], in_=mxs[:, 1:2]).then_inc(st.sem_ve, 1)
                        @block.scalar
                        def _(s):
                            s.wait_ge(st.sem_ve, V + 1)
                            s.activation(out=e_bf[:], in_=s_sb[:], func=AF.Exp,
                                         bias=mxs[:, 0:1]).then_inc(st.sem_ac, 1)
                            for r in range(5):
                                s.wait_ge(st.sem_pe, P + 2 + r)
                                s.activation(out=eT[:, r * 128:(r + 1) * 128],
                                             in_=p_te[:] if r % 2 == 0 else p_tf[:],
                                             func=AF.Copy).then_inc(st.sem_ac, 1)
                            s.wait_ge(st.sem_pe, P + 7)
                            s.wait_ge(st.sem_ve, V + 2)
                            s.activation(out=ctx_tok[:, h * 128:(h + 1) * 128], in_=p_cx[:],
                                         func=AF.Copy,
                                         scale=mxs[:, 2:3]).then_inc(st.sem_ac, 1)
                    st.pe.inc(7); st.ve.inc(2); st.ac.inc(7)
                emit_transpose_set(nc, st, f"f1_a_ct_{tp}",
                                   [(ctx_tok[:, c * 128:(c + 1) * 128],
                                     ctx_fm[:, tp * D + c * 128:tp * D + (c + 1) * 128])
                                    for c in range(HEADS)], p_te, p_tf)

    if dbg:
        _dump(nc, st, "dbg_ctxfm", ctx_fm[:], dbg["ctx_fm"])

    with (
        nc.sbuf_tensor("wo_bf", [128, 2 * NDC * 512], BF16) as wo_bf,
        nc.sbuf_tensor("wstage2", [128, 2 * 512], F32) as wstage2,
        nc.sbuf_tensor("ao", [128, NT_P * D], F32) as ao,
        nc.sbuf_tensor("xres", [128, D], F32) as xres,
        nc.sbuf_tensor("lnst", [128, 8], F32) as lnst,
        nc.psum_tensor("p_o0", [128, 512], F32) as p_o0,
        nc.psum_tensor("p_o1", [128, 512], F32) as p_o1,
    ):
        for opass in range(2):
            jobs = []
            for dp in range(2):
                d0 = (opass * 2 + dp) * 512
                jobs += [(io["woT"][c * 128:(c + 1) * 128, d0:d0 + 512],
                          wo_bf[:, dp * NDC * 512 + c * 512:dp * NDC * 512 + (c + 1) * 512])
                         for c in range(NDC)]
            emit_load_w_bf16(nc, st, f"f1_ldwo_{opass}", jobs, wstage2)
            for tp in range(NT_P):
                ve_war_o, ac_war_o = st.ve.v, st.ac.v
                with nc.Block(f"f1_o_mm_{opass}_{tp}") as block:
                    @block.tensor
                    def _(te):
                        te.wait_ge(st.sem_ve, ve_war_o)
                        te.wait_ge(st.sem_ac, ac_war_o)
                        for c in range(HEADS):
                            cs_ = ctx_fm[:, tp * D + c * 128:tp * D + (c + 1) * 128]
                            te.matmul(p_o0[:], cs_, wo_bf[:, c * 512:(c + 1) * 512],
                                      start=(c == 0), stop=(c == HEADS - 1))
                            ins = te.matmul(p_o1[:], cs_,
                                            wo_bf[:, NDC * 512 + c * 512:NDC * 512 + (c + 1) * 512],
                                            start=(c == 0), stop=(c == HEADS - 1))
                            if c == HEADS - 1:
                                ins.then_inc(st.sem_pe, 1)
                st.pe.inc()
                pe_wo = st.pe.v
                with nc.Block(f"f1_o_ev_{opass}_{tp}") as block:
                    @block.vector
                    def _(v):
                        v.wait_ge(st.sem_pe, pe_wo)
                        v.tensor_copy(out=ao[:, tp * D + opass * 1024:tp * D + opass * 1024 + 512],
                                      in_=p_o0[:]).then_inc(st.sem_ve, 1)
                    @block.scalar
                    def _(s):
                        s.wait_ge(st.sem_pe, pe_wo)
                        s.activation(out=ao[:, tp * D + opass * 1024 + 512:tp * D + opass * 1024 + 1024],
                                     in_=p_o1[:], func=AF.Copy).then_inc(st.sem_ac, 1)
                st.ve.inc(); st.ac.inc()
        for tp in range(NT_P):
            base_ld = st.ld.v
            vs_wr, ac_wr = st.vs.v, st.ac.v
            with nc.Block(f"f1_res_{tp}") as block:
                @block.sync
                def _(sync):
                    sync.wait_ge(st.sem_vs, vs_wr)
                    sync.wait_ge(st.sem_ac, ac_wr)
                    sync.dma_start(out=xres[:],
                                   in_=xh_tok[(tp + HALO_T) * 128:(tp + HALO_T + 1) * 128, :]).then_inc(st.sem_ld, 16)
                @block.vector
                def _(v):
                    v.wait_ge(st.sem_ld, base_ld + 16)
                    v.wait_ge(st.sem_ac, ac_wr)
                    v.tensor_add(ao[:, tp * D:(tp + 1) * D], ao[:, tp * D:(tp + 1) * D],
                                 xres[:]).then_inc(st.sem_vs, 1)
            st.ld.inc(16); st.vs.inc()
            emit_ln(nc, st, f"f1_ln_{tp}", ao[:, tp * D:(tp + 1) * D], lnst, xres[:],
                    out_dram=io["x1_out"][tp * 128:(tp + 1) * 128, :],
                    pre_wait=[(st.sem_vs, st.vs.v)])
    octx.close()


def emit_final_wait(nc, st):
    with nc.Block("final_wait") as block:
        @block.sync
        def _(sync):
            sync.wait_ge(st.sem_st, st.stc.v)
            sync.wait_ge(st.sr[0].sem, st.sr[0].v)
            sync.wait_ge(st.sr[1].sem, st.sr[1].v)



def _dram_in(nc, name, shape):
    return nc.dram_tensor(name, list(shape), F32, kind="ExternalInput").ap()


def build_f1(debug=False):
    nc = bass.Bass("TRN2", target_bir_lowering=False, debug=False)
    io = {
        "xh_tok": _dram_in(nc, "xh_tok", [T_EXT, D]),
        "xh_fmT": _dram_in(nc, "xh_fmT", [D, T_EXT]),
        "wqT": _dram_in(nc, "wqT", [D, D]),
        "wkT": _dram_in(nc, "wkT", [D, 512]),
        "wvT": _dram_in(nc, "wvT", [D, 512]),
        "woT": _dram_in(nc, "woT", [D, D]),
        "qn_w": _dram_in(nc, "qn_w", [DH]),
        "kn_w": _dram_in(nc, "kn_w", [DH]),
        "pos0": _dram_in(nc, "pos0", [1, 1]),
        "kill": _dram_in(nc, "kill", [1, 1]),
        "x1_out": nc.dram_tensor("x1_out", [T_PRIM, D], F32, kind="ExternalOutput").ap(),
    }
    if debug:
        io["dbg"] = {
            "q_fm": nc.dram_tensor("d_qfm", [128, HEADS * T_PRIM], BF16, kind="ExternalOutput").ap(),
            "k_fm": nc.dram_tensor("d_kfm", [128, KVH * T_EXT], BF16, kind="ExternalOutput").ap(),
            "v_tok": nc.dram_tensor("d_vtok", [128, NT_E * 512], BF16, kind="ExternalOutput").ap(),
            "xh_fm": nc.dram_tensor("d_xhfm", [128, NDC * T_EXT], BF16, kind="ExternalOutput").ap(),
            "ctx_fm": nc.dram_tensor("d_ctxfm", [128, NT_P * D], BF16, kind="ExternalOutput").ap(),
        }
    st = St()
    with ExitStack() as ctx:
        emit_consts(nc, ctx, st, io)
        emit_l0_attn(nc, ctx, st, io)
        emit_final_wait(nc, st)
    return nc


def f1_in_maps(inputs):
    x = np.ascontiguousarray(inputs["L0_attn_x"], np.float32)
    wqT = np.ascontiguousarray(inputs["L0_attn_Wq"].T)
    wkT = np.ascontiguousarray(inputs["L0_attn_Wk"].T)
    wvT = np.ascontiguousarray(inputs["L0_attn_Wv"].T)
    woT = np.ascontiguousarray(inputs["L0_attn_Wo"].T)
    maps = []
    for c in range(8):
        b, par = c // 2, c % 2
        if par == 0:
            xh = np.concatenate([x[b, 0:512], x[b, 0:1024]], 0)
        else:
            xh = x[b, 512:2048]
        xh = np.ascontiguousarray(xh)
        maps.append({
            "xh_tok": xh,
            "xh_fmT": np.ascontiguousarray(xh.T),
            "wqT": wqT, "wkT": wkT, "wvT": wvT, "woT": woT,
            "qn_w": np.ascontiguousarray(inputs["L0_attn_qn_w"]),
            "kn_w": np.ascontiguousarray(inputs["L0_attn_kn_w"]),
            "pos0": np.array([[par * 1024 - 512]], np.float32),
            "kill": np.array([[0.0 if par else NEG]], np.float32),
        })
    return maps


def run_f1(inputs, debug=False):
    nc = build_f1(debug)
    res = run_bass_kernel_spmd(nc, f1_in_maps(inputs), core_ids=list(range(8)))
    out = np.zeros((4, 2048, 2048), np.float32)
    for c in range(8):
        b, par = c // 2, c % 2
        out[b, par * 1024:(par + 1) * 1024] = res.results[c]["x1_out"]
    if debug:
        return out, res
    return out




F_FFN = 8192
NFT = F_FFN // 128
NPANEL = 16


def emit_absmax(nc, st, name, jobs, stage, acc_col):
    w = jobs[0].shape[1]
    base_vs = st.vs.v
    base_lr = [st.lr[0].v, st.lr[1].v]
    with nc.Block(name) as block:
        @block.sync
        def _(sync):
            for n, src_ in enumerate(jobs):
                if n < 2:
                    sync.wait_ge(st.sem_vs, base_vs)
                else:
                    sync.wait_ge(st.sem_vs, base_vs + 2 * (n - 1))
                sync.dma_start(out=stage[:, (n % 2) * w:(n % 2 + 1) * w],
                               in_=src_).then_inc(st.lr[n % 2].sem, 16)
        @block.vector
        def _(v):
            for n, src_ in enumerate(jobs):
                v.wait_ge(st.lr[n % 2].sem, base_lr[n % 2] + (n // 2 + 1) * 16)
                if n == 0:
                    v.tensor_reduce(out=acc_col, in_=stage[:, 0:w], axis=AX.X,
                                    op=ALU.max, apply_absolute_value=True).then_inc(st.sem_vs, 1)
                    v.wait_ge(st.sem_vs, st.vs.inc())
                    v.engine_nop().then_inc(st.sem_vs, 1)
                    v.wait_ge(st.sem_vs, st.vs.inc())
                else:
                    v.tensor_reduce(out=st.cpool[:, 7:8],
                                    in_=stage[:, (n % 2) * w:(n % 2 + 1) * w],
                                    axis=AX.X, op=ALU.max,
                                    apply_absolute_value=True).then_inc(st.sem_vs, 1)
                    v.wait_ge(st.sem_vs, st.vs.inc())
                    v.tensor_max(acc_col, acc_col, st.cpool[:, 7:8]).then_inc(st.sem_vs, 1)
                    v.wait_ge(st.sem_vs, st.vs.inc())
    nj = len(jobs)
    st.lr[0].inc(16 * ((nj + 1) // 2)); st.lr[1].inc(16 * (nj // 2))


def emit_l0_ffn(nc, ctx, st, io):
    x1, w1T, w2T = io["x1"], io["w1T"], io["w2T"]
    scr = io["scr_dram"]
    dbg = io.get("dbg")

    with ExitStack() as fctx:
        r1_fm = fctx.enter_context(nc.sbuf_tensor("r1_fm", [128, NDC * T_PRIM], BF16))
        out2 = fctx.enter_context(nc.sbuf_tensor("out2", [128, NT_P * D], F32))
        qsc = fctx.enter_context(nc.sbuf_tensor("qsc", [128, 8], F32))
        qrow = fctx.enter_context(nc.sbuf_tensor("qrow", [1, 16], F32))
        mxacc = fctx.enter_context(nc.sbuf_tensor("mxacc", [128, 2], F32))
        b1q = fctx.enter_context(nc.sbuf_tensor("b1q", [128, 64], F32))
        b2qb = fctx.enter_context(nc.sbuf_tensor("b2qb", [128, D], F32))
        b2row = fctx.enter_context(nc.sbuf_tensor("b2row", [1, D], F32))
        b2i = fctx.enter_context(nc.sbuf_tensor("b2i", [1, D], I32))
        b1s = fctx.enter_context(nc.sbuf_tensor("b1s", [64, 128], F32))
        b1i = fctx.enter_context(nc.sbuf_tensor("b1i", [128, 64], I32))
        rowmx = fctx.enter_context(nc.sbuf_tensor("rowmx", [1, 256], F32))

        with nc.Block("f3_pre") as block:
            @block.sync
            def _(sync):
                sync.dma_start(out=qrow[:, 0:1], in_=io["s1"]).then_inc(st.sem_ld, 16)
                sync.dma_start(out=qrow[:, 1:2], in_=io["s2"]).then_inc(st.sem_ld, 16)
                sync.dma_start(out=b1s[:], in_=io["b1"].rearrange("(a b) -> a b", a=64)).then_inc(st.sem_ld, 16)
                sync.dma_start(out=b2row[:], in_=io["b2"].unsqueeze(0)).then_inc(st.sem_ld, 16)
        st.ld.inc(64)

        with nc.sbuf_tensor("wstg3", [128, 2 * 4096], F32) as wstg3:
            emit_absmax(nc, st, "f3_mx1",
                        [w1T[c * 128:(c + 1) * 128, h * 4096:(h + 1) * 4096]
                         for c in range(NDC) for h in range(2)],
                        wstg3, mxacc[:, 0:1])
            emit_absmax(nc, st, "f3_mx2",
                        [w2T[c * 128:(c + 1) * 128, 0:2048].rearrange("p (a b) -> p a b", a=1).squeeze(1)
                         if False else w2T[c * 128:(c + 1) * 128, :]
                         for c in range(NFT)],
                        wstg3[:, 0:2 * 2048], mxacc[:, 1:2])

        with nc.Block("f3_pr1") as block:
            vsnow = st.vs.v
            @block.sync
            def _(sync):
                sync.wait_ge(st.sem_vs, vsnow)
                sync.dma_start(out=scr[0:128], in_=mxacc[:, 0:1]).then_inc(st.sem_ld, 16)
                sync.dma_start(out=scr[128:256], in_=mxacc[:, 1:2]).then_inc(st.sem_ld, 16)
        st.ld.inc(32)
        with nc.Block("f3_pr2") as block:
            base_ld2 = st.ld.v
            @block.sync
            def _(sync):
                sync.wait_ge(st.sem_ld, base_ld2)
                sync.dma_start(out=rowmx[:], in_=scr.unsqueeze(0)).then_inc(st.sem_ld, 16)
        st.ld.inc(16)
        with nc.Block("f3_sc") as block:
            @block.vector
            def _(v):
                def vsync(inst):
                    inst.then_inc(st.sem_vs, 1)
                    v.wait_ge(st.sem_vs, st.vs.inc())
                v.wait_ge(st.sem_ld, st.ld.v)
                vsync(v.reduce_max(out=qrow[:, 2:3], in_=rowmx[:, 0:128], axis=AX.X))
                vsync(v.reduce_max(out=qrow[:, 3:4], in_=rowmx[:, 128:256], axis=AX.X))
                vsync(v.reciprocal(out=qrow[:, 4:5], in_=qrow[:, 0:1]))
                vsync(v.reciprocal(out=qrow[:, 12:13], in_=qrow[:, 2:3]))
                vsync(v.tensor_scalar(out=qrow[:, 5:6], in0=qrow[:, 12:13], scalar1=7.0,
                                      scalar2=None, op0=ALU.mult))
                vsync(v.tensor_mul(qrow[:, 13:14], qrow[:, 0:1], qrow[:, 2:3]))
                vsync(v.tensor_scalar(out=qrow[:, 6:7], in0=qrow[:, 13:14], scalar1=1.0 / 7.0,
                                      scalar2=None, op0=ALU.mult))
                vsync(v.tensor_mul(qrow[:, 7:8], qrow[:, 4:5], qrow[:, 5:6]))
                vsync(v.reciprocal(out=qrow[:, 8:9], in_=qrow[:, 1:2]))
                vsync(v.reciprocal(out=qrow[:, 12:13], in_=qrow[:, 3:4]))
                vsync(v.tensor_scalar(out=qrow[:, 9:10], in0=qrow[:, 12:13], scalar1=7.0,
                                      scalar2=None, op0=ALU.mult))
                vsync(v.tensor_mul(qrow[:, 13:14], qrow[:, 1:2], qrow[:, 3:4]))
                vsync(v.tensor_scalar(out=qrow[:, 10:11], in0=qrow[:, 13:14], scalar1=1.0 / 7.0,
                                      scalar2=None, op0=ALU.mult))
                vsync(v.tensor_mul(qrow[:, 11:12], qrow[:, 8:9], qrow[:, 9:10]))
        with nc.psum_tensor("p_q", [128, 128], F32) as p_q:
            with nc.Block("f3_bc1") as block:
                @block.tensor
                def _(t):
                    t.matmul(p_q[:, 0:8], st.ones_f[:], qrow[:, 4:12],
                             start=True, stop=True).then_inc(st.sem_pe, 1)
            st.pe.inc()
            pe_bc = st.pe.v
            with nc.Block("f3_bc2") as block:
                @block.vector
                def _(v):
                    v.wait_ge(st.sem_pe, pe_bc)
                    v.tensor_copy(out=qsc[:], in_=p_q[:, 0:8])

            with nc.Block("f3_b1t") as block:
                base_ld3 = st.ld.v
                @block.tensor
                def _(t):
                    t.wait_ge(st.sem_ld, base_ld3)
                    t.transpose(p_q[:, 0:64], b1s[:],
                                st.ident_f[0:64, 0:64]).then_inc(st.sem_pe, 1)
            st.pe.inc()
            pe_b1 = st.pe.v
            with nc.Block("f3_b1q") as block:
                @block.vector
                def _(v):
                    v.wait_ge(st.sem_pe, pe_b1)
                    def vsync(inst):
                        inst.then_inc(st.sem_vs, 1)
                        v.wait_ge(st.sem_vs, st.vs.inc())
                    vsync(v.tensor_scalar(out=b1i[:], in0=p_q[:, 0:64], scalar1=qsc[:, 3:4],
                                          scalar2=None, op0=ALU.mult))
                    vsync(v.tensor_copy(out=b1q[:], in_=b1i[:]))
                    v.tensor_scalar(out=b1q[:], in0=b1q[:], scalar1=qsc[:, 2:3],
                                    scalar2=None, op0=ALU.mult)
            with nc.Block("f3_b2q") as block:
                @block.vector
                def _(v):
                    def vsync(inst):
                        inst.then_inc(st.sem_vs, 1)
                        v.wait_ge(st.sem_vs, st.vs.inc())
                    vsync(v.tensor_scalar(out=b2i[:], in0=b2row[:], scalar1=qrow[:, 11:12],
                                          scalar2=None, op0=ALU.mult))
                    vsync(v.tensor_copy(out=b2row[:], in_=b2i[:]))
                    v.tensor_scalar(out=b2row[:], in0=b2row[:], scalar1=qrow[:, 10:11],
                                    scalar2=None, op0=ALU.mult).then_inc(st.sem_vs, 1)
            st.vs.inc()
            b2row_vs = st.vs.v
            for i in range(4):
                with nc.Block(f"f3_b2b_{i}") as block:
                    @block.tensor
                    def _(t):
                        t.wait_ge(st.sem_vs, b2row_vs)
                        t.matmul(p_q[:], st.ones_f[:], b2row[:, i * 512:i * 512 + 128],
                                 start=True, stop=True).then_inc(st.sem_pe, 1)
                st.pe.inc()
                pe_b2 = st.pe.v
                with nc.Block(f"f3_b2c_{i}") as block:
                    @block.vector
                    def _(v):
                        v.wait_ge(st.sem_pe, pe_b2)
                        v.tensor_copy(out=b2qb[:, i * 512:i * 512 + 128], in_=p_q[:])
        with nc.psum_tensor("p_q5", [128, 512], F32) as p_q5:
            for i in range(4):
                with nc.Block(f"f3_b2d_{i}") as block:
                    @block.tensor
                    def _(t):
                        t.wait_ge(st.sem_vs, b2row_vs)
                        t.matmul(p_q5[:], st.ones_f[:], b2row[:, i * 512:(i + 1) * 512],
                                 start=True, stop=True).then_inc(st.sem_pe, 1)
                st.pe.inc()
                pe_b2e = st.pe.v
                with nc.Block(f"f3_b2e_{i}") as block:
                    @block.vector
                    def _(v):
                        v.wait_ge(st.sem_pe, pe_b2e)
                        v.tensor_copy(out=b2qb[:, i * 512:(i + 1) * 512], in_=p_q5[:])

        with (
            nc.sbuf_tensor("x1t", [128, D], F32) as x1t,
            nc.sbuf_tensor("r1i", [128, D], I32) as r1i,
            nc.sbuf_tensor("r1b", [128, D], BF16) as r1b,
            nc.psum_tensor("p_t0", [128, 128], BF16) as p_t0,
            nc.psum_tensor("p_t1", [128, 128], BF16) as p_t1,
        ):
            for tp in range(NT_P):
                base_ld = st.ld.v
                vs_w3, pe_w3 = st.vs.v, st.pe.v
                st_w3 = (st.stc.v, st.sr[0].v, st.sr[1].v)
                with nc.Block(f"f3_r1_{tp}") as block:
                    @block.sync
                    def _(sync):
                        sync.wait_ge(st.sem_st, st_w3[0])
                        sync.wait_ge(st.sem_vs, vs_w3)
                        sync.wait_ge(st.sem_pe, pe_w3)
                        sync.dma_start(out=x1t[:], in_=x1[tp * 128:(tp + 1) * 128, :]).then_inc(st.sem_ld, 16)
                    @block.vector
                    def _(v):
                        def vsync(inst):
                            inst.then_inc(st.sem_vs, 1)
                            v.wait_ge(st.sem_vs, st.vs.inc())
                        v.wait_ge(st.sem_ld, base_ld + 16)
                        vsync(v.tensor_scalar(out=r1i[:], in0=x1t[:], scalar1=qsc[:, 0:1],
                                              scalar2=None, op0=ALU.mult))
                        vsync(v.tensor_scalar(out=r1i[:], in0=r1i[:], scalar1=-8,
                                              scalar2=7, op0=ALU.max, op1=ALU.min))
                        v.tensor_copy(out=r1b[:], in_=r1i[:]).then_inc(st.sem_vs, 1)
                st.ld.inc(16); st.vs.inc()
                emit_transpose_set(nc, st, f"f3_r1t_{tp}",
                                   [(r1b[:, c * 128:(c + 1) * 128],
                                     r1_fm[:, c * T_PRIM + tp * 128:c * T_PRIM + (tp + 1) * 128])
                                    for c in range(NDC)], p_t0, p_t1,
                                   pre_wait=[(st.sem_vs, st.vs.v)])

        if dbg:
            _dump(nc, st, "f3_dbg_r1", r1_fm[:], dbg["r1_fm"])
            _dump(nc, st, "f3_dbg_qsc", qsc[:], dbg["qsc"])
            _dump(nc, st, "f3_dbg_b1q", b1q[:], dbg["b1q"])
            _dump(nc, st, "f3_dbg_b2q", b2qb[:], dbg["b2qb"])
        with nc.Block("f3_z") as block:
            @block.gpsimd
            def _(g):
                g.memset(out2[:], 0.0)

        with (
            nc.sbuf_tensor("m1", [128, NDC * 512], BF16) as m1,
            nc.sbuf_tensor("m2", [128, 4 * D], BF16) as m2,
            nc.sbuf_tensor("wstg", [128, 2 * 2048], F32) as wstg,
            nc.sbuf_tensor("wqi", [128, 2048], I32) as wqi,
            nc.sbuf_tensor("rh", [128, 2 * 1024], F32) as rh,
            nc.sbuf_tensor("r2i", [128, 1024], I32) as r2i,
            nc.sbuf_tensor("r2p", [128, 4 * T_PRIM], BF16) as r2p,
            nc.psum_tensor("p_h0", [128, 512], F32) as p_h0,
            nc.psum_tensor("p_h1", [128, 512], F32) as p_h1,
            nc.psum_tensor("p_h2", [128, 512], F32) as p_h2,
            nc.psum_tensor("p_h3", [128, 512], F32) as p_h3,
            nc.psum_tensor("p_oa", [128, 1024], F32) as p_oa,
            nc.psum_tensor("p_ob", [128, 1024], F32) as p_ob,
        ):
            for panel in range(NPANEL):
                f0 = panel * 512
                base_vs = st.vs.v
                base_lr = [st.lr[0].v, st.lr[1].v]
                with nc.Block(f"f3_w1_{panel}") as block:
                    @block.sync
                    def _(sync):
                        for c in range(NDC):
                            if c < 2:
                                sync.wait_ge(st.sem_vs, base_vs)
                            else:
                                sync.wait_ge(st.sem_vs, base_vs + 2 * (c - 1))
                            sync.dma_start(out=wstg[:, (c % 2) * 512:(c % 2) * 512 + 512],
                                           in_=w1T[c * 128:(c + 1) * 128, f0:f0 + 512]).then_inc(st.lr[c % 2].sem, 16)
                    @block.vector
                    def _(v):
                        for c in range(NDC):
                            v.wait_ge(st.lr[c % 2].sem, base_lr[c % 2] + (c // 2 + 1) * 16)
                            v.tensor_scalar(out=wqi[:, 0:512], in0=wstg[:, (c % 2) * 512:(c % 2) * 512 + 512],
                                            scalar1=qsc[:, 1:2], scalar2=None,
                                            op0=ALU.mult).then_inc(st.sem_vs, 1)
                            v.wait_ge(st.sem_vs, st.vs.inc())
                            v.tensor_copy(out=m1[:, c * 512:(c + 1) * 512],
                                          in_=wqi[:, 0:512]).then_inc(st.sem_vs, 1)
                            v.wait_ge(st.sem_vs, st.vs.inc())
                st.lr[0].inc(16 * 8); st.lr[1].inc(16 * 8)
                m1_vs = st.vs.v
                hp = [(p_h0, p_h1), (p_h2, p_h3)]
                base_pe, base_ac, base_vs2 = st.pe.v, st.ac.v, st.vs.v
                with nc.Block(f"f3_h_{panel}") as block:
                    @block.tensor
                    def _(te):
                        te.wait_ge(st.sem_vs, m1_vs)
                        for ft in range(4):
                            pa, pb = hp[ft % 2]
                            if ft >= 2:
                                te.wait_ge(st.sem_ac, base_ac + 2 * (ft - 1))
                            else:
                                te.wait_ge(st.sem_ac, base_ac)
                            for c in range(NDC):
                                lh = m1[:, c * 512 + ft * 128:c * 512 + (ft + 1) * 128]
                                te.matmul(pa[:], lh, r1_fm[:, c * T_PRIM:c * T_PRIM + 512],
                                          start=(c == 0), stop=(c == NDC - 1))
                                ins = te.matmul(pb[:], lh, r1_fm[:, c * T_PRIM + 512:c * T_PRIM + 1024],
                                                start=(c == 0), stop=(c == NDC - 1))
                                if c == NDC - 1:
                                    ins.then_inc(st.sem_pe, 1)
                    @block.scalar
                    def _(s):
                        for ft in range(4):
                            pa, pb = hp[ft % 2]
                            col = panel * 4 + ft
                            s.wait_ge(st.sem_pe, base_pe + ft + 1)
                            if ft >= 2:
                                s.wait_ge(st.sem_vs, base_vs2 + 3 * (ft - 1))
                            rsl = rh[:, (ft % 2) * 1024:(ft % 2) * 1024 + 1024]
                            s.activation(out=rsl[:, 0:512] if False else rh[:, (ft % 2) * 1024:(ft % 2) * 1024 + 512],
                                         in_=pa[:], func=AF.Relu,
                                         scale=qsc[:, 2:3], bias=b1q[:, col:col + 1]).then_inc(st.sem_ac, 1)
                            s.activation(out=rh[:, (ft % 2) * 1024 + 512:(ft % 2) * 1024 + 1024],
                                         in_=pb[:], func=AF.Relu,
                                         scale=qsc[:, 2:3], bias=b1q[:, col:col + 1]).then_inc(st.sem_ac, 1)
                    @block.vector
                    def _(v):
                        for ft in range(4):
                            v.wait_ge(st.sem_ac, base_ac + 2 * (ft + 1))
                            v.tensor_scalar(out=r2i[:], in0=rh[:, (ft % 2) * 1024:(ft % 2) * 1024 + 1024],
                                            scalar1=qsc[:, 4:5],
                                            scalar2=None, op0=ALU.mult).then_inc(st.sem_vs, 1)
                            v.wait_ge(st.sem_vs, st.vs.inc())
                            v.tensor_scalar(out=r2i[:], in0=r2i[:], scalar1=0,
                                            scalar2=15, op0=ALU.max, op1=ALU.min).then_inc(st.sem_vs, 1)
                            v.wait_ge(st.sem_vs, st.vs.inc())
                            v.tensor_copy(out=r2p[:, ft * T_PRIM:(ft + 1) * T_PRIM],
                                          in_=r2i[:]).then_inc(st.sem_vs, 1)
                            st.vs.inc()
                st.pe.inc(4); st.ac.inc(8)
                base_vs = st.vs.v
                base_lr = [st.lr[0].v, st.lr[1].v]
                with nc.Block(f"f3_w2_{panel}") as block:
                    @block.sync
                    def _(sync):
                        for c in range(4):
                            if c < 2:
                                sync.wait_ge(st.sem_vs, base_vs)
                            else:
                                sync.wait_ge(st.sem_vs, base_vs + 2 * (c - 1))
                            sync.dma_start(out=wstg[:, (c % 2) * 2048:(c % 2) * 2048 + 2048],
                                           in_=w2T[f0 + c * 128:f0 + (c + 1) * 128, :]).then_inc(st.lr[c % 2].sem, 16)
                    @block.vector
                    def _(v):
                        for c in range(4):
                            v.wait_ge(st.lr[c % 2].sem, base_lr[c % 2] + (c // 2 + 1) * 16)
                            v.tensor_scalar(out=wqi[:], in0=wstg[:, (c % 2) * 2048:(c % 2) * 2048 + 2048],
                                            scalar1=qsc[:, 5:6], scalar2=None,
                                            op0=ALU.mult).then_inc(st.sem_vs, 1)
                            v.wait_ge(st.sem_vs, st.vs.inc())
                            v.tensor_copy(out=m2[:, c * 2048:(c + 1) * 2048],
                                          in_=wqi[:]).then_inc(st.sem_vs, 1)
                            v.wait_ge(st.sem_vs, st.vs.inc())
                st.lr[0].inc(16 * 2); st.lr[1].inc(16 * 2)
                m2r2_vs = st.vs.v
                base_pe_o, base_ve_o = st.pe.v, st.ve.v
                with nc.Block(f"f3_o_{panel}") as block:
                    @block.tensor
                    def _(te):
                        te.wait_ge(st.sem_vs, m2r2_vs)
                        for gix in range(2 * NT_P):
                            tt, dh = gix // 2, gix % 2
                            po = p_oa if gix % 2 == 0 else p_ob
                            if gix >= 2:
                                te.wait_ge(st.sem_ve, base_ve_o + gix - 1)
                            for fc in range(4):
                                lh = r2p[:, fc * T_PRIM + tt * 128:fc * T_PRIM + (tt + 1) * 128]
                                for dp in range(2):
                                    d0 = dh * 2 + dp
                                    ins = te.matmul(po[:, dp * 512:(dp + 1) * 512], lh,
                                                    m2[:, fc * 2048 + d0 * 512:fc * 2048 + (d0 + 1) * 512],
                                                    start=(fc == 0), stop=(fc == 3))
                                    if fc == 3 and dp == 1:
                                        ins.then_inc(st.sem_pe, 1)
                    @block.vector
                    def _(v):
                        for gix in range(2 * NT_P):
                            tt, dh = gix // 2, gix % 2
                            po = p_oa if gix % 2 == 0 else p_ob
                            v.wait_ge(st.sem_pe, base_pe_o + gix + 1)
                            v.tensor_add(out2[:, tt * D + dh * 1024:tt * D + (dh + 1) * 1024],
                                         out2[:, tt * D + dh * 1024:tt * D + (dh + 1) * 1024],
                                         po[:]).then_inc(st.sem_ve, 1)
                st.pe.inc(2 * NT_P); st.ve.inc(2 * NT_P)

        with (
            nc.sbuf_tensor("x1t2", [128, D], F32) as x1t2,
            nc.sbuf_tensor("lnst2", [128, 8], F32) as lnst2,
            nc.sbuf_tensor("lnsc", [128, D], F32) as lnsc,
        ):
            for tp in range(NT_P):
                base_ld = st.ld.v
                vs_wf = st.vs.v
                st_wf = st.stc.v
                with nc.Block(f"f3_fin_{tp}") as block:
                    @block.sync
                    def _(sync):
                        sync.wait_ge(st.sem_st, st_wf)
                        sync.wait_ge(st.sem_vs, vs_wf)
                        sync.dma_start(out=x1t2[:], in_=x1[tp * 128:(tp + 1) * 128, :]).then_inc(st.sem_ld, 16)
                    @block.vector
                    def _(v):
                        def vsync(inst):
                            inst.then_inc(st.sem_vs, 1)
                            v.wait_ge(st.sem_vs, st.vs.inc())
                        vsync(v.tensor_scalar(out=out2[:, tp * D:(tp + 1) * D],
                                              in0=out2[:, tp * D:(tp + 1) * D],
                                              scalar1=qsc[:, 6:7], scalar2=None, op0=ALU.mult))
                        vsync(v.tensor_add(out2[:, tp * D:(tp + 1) * D],
                                           out2[:, tp * D:(tp + 1) * D], b2qb[:]))
                        v.wait_ge(st.sem_ld, base_ld + 16)
                        v.tensor_add(out2[:, tp * D:(tp + 1) * D],
                                     out2[:, tp * D:(tp + 1) * D],
                                     x1t2[:]).then_inc(st.sem_vs, 1)
                st.ld.inc(16); st.vs.inc()
                emit_ln(nc, st, f"f3_ln_{tp}", out2[:, tp * D:(tp + 1) * D], lnst2, lnsc[:],
                        out_dram=io["x2_out"][tp * 128:(tp + 1) * 128, :],
                        pre_wait=[(st.sem_vs, st.vs.v)])




BS_LA = 256
NCH = T_PRIM // BS_LA


def emit_l1_local(nc, ctx, st, io, skip_scan=False, prefix=None):
    x2 = io["x2"]
    q_d, k_d, v_d, g_d = io["q_tok_d"], io["k_tok_d"], io["v_tok_d"], io["g_tok_d"]

    with ExitStack() as fctx:
        kv = fctx.enter_context(nc.sbuf_tensor("kv", [128, NDC * D], BF16))
        maskA = fctx.enter_context(nc.sbuf_tensor("maskA", [128, 256], F32))
        maskB = fctx.enter_context(nc.sbuf_tensor("maskB", [128, 256], F32))
        prjctx = fctx.enter_context(ExitStack())
        xn_fm = prjctx.enter_context(nc.sbuf_tensor("xn_fm", [128, NDC * T_PRIM], BF16))

        with nc.Block("f4_cst") as block:
            @block.gpsimd
            def _(g):
                g.memset(maskA[:], 1.0)
                g.memset(maskB[:], 1.0)
                if prefix is None:
                    g.memset(kv[:], 0.0)
        with nc.Block("f4_cst2") as block:
            @block.gpsimd
            def _(g):
                g.affine_select(out=maskA[:, 0:128], in_=maskA[:, 0:128], pattern=[[1, 128]],
                                compare_op=ALU.is_ge, fill=0.0, channel_multiplier=-1)
                g.affine_select(out=maskB[:, 128:256], in_=maskB[:, 128:256], pattern=[[1, 128]],
                                compare_op=ALU.is_ge, fill=0.0, channel_multiplier=-1)
                g.memset(maskB[:, 0:128], 0.0)

        with (
            nc.sbuf_tensor("x2t", [128, D], F32) as x2t,
            nc.sbuf_tensor("sqs", [128, D], F32) as sqs,
            nc.sbuf_tensor("xnb", [128, D], BF16) as xnb,
            nc.sbuf_tensor("rms_s", [128, 4], F32) as rms_s,
            nc.psum_tensor("p_t0b", [128, 128], BF16) as p_t0,
            nc.psum_tensor("p_t1b", [128, 128], BF16) as p_t1,
        ):
            for tp in range(NT_P):
                base_ld, base_ac = st.ld.v, st.ac.v
                vs_w4, pe_w4 = st.vs.v, st.pe.v
                st_w4 = st.stc.v
                with nc.Block(f"f4_xn_{tp}") as block:
                    @block.vector
                    def _(v):
                        def vsync(inst):
                            inst.then_inc(st.sem_vs, 1)
                            v.wait_ge(st.sem_vs, st.vs.inc())
                        v.wait_ge(st.sem_ld, base_ld + 16)
                        vsync(v.tensor_mul(sqs[:], x2t[:], x2t[:]))
                        v.reduce_sum(out=rms_s[:, 0:1], in_=sqs[:], axis=AX.X).then_inc(st.sem_ve, 1)
                        v.wait_ge(st.sem_ac, base_ac + 1)
                        v.tensor_scalar(out=xnb[:], in0=x2t[:], scalar1=rms_s[:, 2:3],
                                        scalar2=None, op0=ALU.mult).then_inc(st.sem_vs, 1)
                    @block.scalar
                    def _(s):
                        s.wait_ge(st.sem_ve, st.ve.v + 1)
                        s.activation(out=rms_s[:, 1:2], in_=rms_s[:, 0:1], func=AF.Ln,
                                     scale=1.0 / D, bias=EPS_RMS).then_inc(st.sem_ss, 1)
                        s.wait_ge(st.sem_ss, st.ss_c.inc())
                        s.activation(out=rms_s[:, 2:3], in_=rms_s[:, 1:2], func=AF.Exp,
                                     scale=-0.5).then_inc(st.sem_ac, 1)
                    @block.sync
                    def _(sync):
                        sync.wait_ge(st.sem_st, st_w4)
                        sync.wait_ge(st.sem_vs, vs_w4)
                        sync.wait_ge(st.sem_pe, pe_w4)
                        sync.dma_start(out=x2t[:], in_=x2[tp * 128:(tp + 1) * 128, :]).then_inc(st.sem_ld, 16)
                st.ld.inc(16); st.ve.inc(1); st.ac.inc(1); st.vs.inc()
                emit_transpose_set(nc, st, f"f4_xnt_{tp}",
                                   [(xnb[:, c * 128:(c + 1) * 128],
                                     xn_fm[:, c * T_PRIM + tp * 128:c * T_PRIM + (tp + 1) * 128])
                                    for c in range(NDC)], p_t0, p_t1,
                                   pre_wait=[(st.sem_vs, st.vs.v)])

        with (
            nc.sbuf_tensor("wpbuf", [128, 2 * NDC * 512], BF16) as wpbuf,
            nc.sbuf_tensor("prstg", [128, 2 * 512], BF16) as prstg,
            nc.sbuf_tensor("wldst", [128, 2 * 512], F32) as wldst,
            nc.psum_tensor("p_pr0", [128, 512], F32) as p_pr0,
            nc.psum_tensor("p_pr1", [128, 512], F32) as p_pr1,
        ):
            wlist = [(io["l1qT"], AF.Silu, q_d), (io["l1kT"], AF.Silu, k_d),
                     (io["l1vT"], AF.Silu, v_d), (io["l1gT"], AF.Sigmoid, g_d)]
            panels = [(wi, p) for wi in range(4) for p in range(4)]
            pe_hist = {-1: st.pe.v, -2: st.pe.v}
            w_ve_of = {}

            def load_panel(i):
                wi, p = panels[i]
                wT = wlist[wi][0]
                f0 = p * 512
                slot = i % 2
                emit_load_w_bf16(nc, st, f"f4_pw_{i}",
                                 [(wT[c * 128:(c + 1) * 128, f0:f0 + 512],
                                   wpbuf[:, slot * NDC * 512 + c * 512:slot * NDC * 512 + (c + 1) * 512])
                                  for c in range(NDC)],
                                 wldst, pre_wait=[(st.sem_pe, pe_hist[i - 2])])
                w_ve_of[i] = st.ve.v

            load_panel(0)
            for i in range(len(panels)):
                if i + 1 < len(panels):
                    load_panel(i + 1)
                wi, p = panels[i]
                act, dst = wlist[wi][1], wlist[wi][2]
                f0 = p * 512
                slot = i % 2
                base_pe, base_ac = st.pe.v, st.ac.v
                sr_base = [st.sr[0].v, st.sr[1].v]
                with nc.Block(f"f4_pp_{i}") as block:
                    @block.tensor
                    def _(te):
                        te.wait_ge(st.sem_ve, w_ve_of[i])
                        for tp in range(NT_P):
                            pp = p_pr0 if tp % 2 == 0 else p_pr1
                            if tp >= 2:
                                te.wait_ge(st.sem_ac, base_ac + tp - 1)
                            else:
                                te.wait_ge(st.sem_ac, base_ac)
                            for c in range(NDC):
                                ins = te.matmul(pp[:],
                                                xn_fm[:, c * T_PRIM + tp * 128:c * T_PRIM + (tp + 1) * 128],
                                                wpbuf[:, slot * NDC * 512 + c * 512:slot * NDC * 512 + (c + 1) * 512],
                                                start=(c == 0), stop=(c == NDC - 1))
                                if c == NDC - 1:
                                    ins.then_inc(st.sem_pe, 1)
                    @block.scalar
                    def _(s):
                        for tp in range(NT_P):
                            pp = p_pr0 if tp % 2 == 0 else p_pr1
                            sslot = tp % 2
                            s.wait_ge(st.sem_pe, base_pe + tp + 1)
                            s.wait_ge(st.sr[sslot].sem, sr_base[sslot] + max(0, (tp - 2) // 2 + 1) * 16
                                      if tp >= 2 else sr_base[sslot])
                            s.activation(out=prstg[:, sslot * 512:(sslot + 1) * 512],
                                         in_=pp[:], func=act).then_inc(st.sem_ac, 1)
                    @block.sync
                    def _(sync):
                        for tp in range(NT_P):
                            sslot = tp % 2
                            sync.wait_ge(st.sem_ac, base_ac + tp + 1)
                            sync.dma_start(out=dst[tp * 128:(tp + 1) * 128, f0:f0 + 512],
                                           in_=prstg[:, sslot * 512:(sslot + 1) * 512]).then_inc(st.sr[sslot].sem, 16)
                st.pe.inc(NT_P); st.ac.inc(NT_P)
                st.sr[0].inc(16 * 4); st.sr[1].inc(16 * 4)
                pe_hist[i] = st.pe.v

        prjctx.close()

        if prefix is not None:
            kp_d, vp_d, parb_pre = prefix
            with (
                nc.sbuf_tensor("kp_ch", [128, 2 * D], BF16) as kp_ch,
                nc.sbuf_tensor("vp_ch", [128, 2 * D], BF16) as vp_ch,
                nc.psum_tensor("p_kp", [128, 512], F32) as p_kp,
            ):
                first = True
                for cix in range(NCH):
                    base_ld = st.ld.v
                    pe_w, ve_w = st.pe.v, st.ve.v
                    sr_wp = (st.sr[0].v, st.sr[1].v)
                    with nc.Block(f"f4p_ld_{cix}") as block:
                        @block.sync
                        def _(sync):
                            sync.wait_ge(st.sem_pe, pe_w)
                            sync.wait_ge(st.sem_ve, ve_w)
                            sync.wait_ge(st.sr[0].sem, sr_wp[0])
                            sync.wait_ge(st.sr[1].sem, sr_wp[1])
                            for ti in range(2):
                                row = (cix * 2 + ti) * 128
                                sync.dma_start(out=kp_ch[:, ti * D:(ti + 1) * D],
                                               in_=kp_d[row:row + 128, :]).then_inc(st.sem_ld, 16)
                                sync.dma_start(out=vp_ch[:, ti * D:(ti + 1) * D],
                                               in_=vp_d[row:row + 128, :]).then_inc(st.sem_ld, 16)
                    st.ld.inc(64)
                    ld_now = st.ld.v
                    with nc.Block(f"f4p_w_{cix}") as block:
                        @block.tensor
                        def _(te):
                            te.wait_ge(st.sem_ld, ld_now)
                    for dt in range(NDC):
                        for ep in range(4):
                            ve_war = st.ve.v
                            with nc.Block(f"f4p_kv_{cix}_{dt}_{ep}") as block:
                                @block.tensor
                                def _(te):
                                    te.wait_ge(st.sem_ve, ve_war)
                                    for jt in range(2):
                                        ins = te.matmul(p_kp[:],
                                                        kp_ch[:, jt * D + dt * 128:jt * D + (dt + 1) * 128],
                                                        vp_ch[:, jt * D + ep * 512:jt * D + (ep + 1) * 512],
                                                        start=(jt == 0), stop=(jt == 1))
                                        if jt == 1:
                                            ins.then_inc(st.sem_pe, 1)
                            st.pe.inc()
                            pe_now = st.pe.v
                            with nc.Block(f"f4p_kve_{cix}_{dt}_{ep}") as block:
                                @block.vector
                                def _(v):
                                    v.wait_ge(st.sem_pe, pe_now)
                                    sl = kv[:, dt * D + ep * 512:dt * D + (ep + 1) * 512]
                                    if cix == 0:
                                        v.tensor_copy(out=sl, in_=p_kp[:]).then_inc(st.sem_ve, 1)
                                    else:
                                        v.tensor_tensor(out=sl, in0=p_kp[:], in1=sl,
                                                        op=ALU.add).then_inc(st.sem_ve, 1)
                            st.ve.inc()
                ve_now = st.ve.v
                with nc.Block("f4p_par") as block:
                    @block.vector
                    def _(v):
                        v.wait_ge(st.sem_ve, ve_now)
                        for dt in range(NDC):
                            v.tensor_scalar(out=kv[:, dt * D:(dt + 1) * D],
                                            in0=kv[:, dt * D:(dt + 1) * D],
                                            scalar1=parb_pre[:], scalar2=None,
                                            op0=ALU.mult).then_inc(st.sem_vs, 1)
                            st.vs.inc()
                            v.wait_ge(st.sem_vs, st.vs.v)

        o_loc = fctx.enter_context(nc.sbuf_tensor("o_loc", [128, NT_P * D], F32))
        if skip_scan:
            ac_now, ve_now = st.ac.v, st.ve.v
            with nc.Block("f4_st0") as block:
                @block.sync
                def _(sync):
                    sync.wait_ge(st.sem_ac, ac_now)
                    sync.wait_ge(st.sem_ve, ve_now)
                    for tp in range(NT_P):
                        sync.dma_start(out=io["o_loc_d"][tp * 128:(tp + 1) * 128, :],
                                       in_=o_loc[:, tp * D:(tp + 1) * D]).then_inc(st.sem_st, 16)
                    for c in range(NDC):
                        sync.dma_start(out=io["kv_out"][:, c * D:(c + 1) * D],
                                       in_=kv[:, c * D:(c + 1) * D]).then_inc(st.sem_st, 16)
            st.stc.inc(16 * (NT_P + NDC))
            return
        with (
            nc.sbuf_tensor("q_ch", [128, 2 * D], BF16) as q_ch,
            nc.sbuf_tensor("k_ch", [128, 2 * D], BF16) as k_ch,
            nc.sbuf_tensor("v_ch", [128, 2 * D], BF16) as v_ch,
            nc.sbuf_tensor("qf_ch", [128, NDC * BS_LA], BF16) as qf_ch,
            nc.sbuf_tensor("kf_ch", [128, NDC * BS_LA], BF16) as kf_ch,
            nc.sbuf_tensor("sTm", [128, 2 * BS_LA], BF16) as sTm,
            nc.psum_tensor("p_sT", [128, 256], F32) as p_sT,
            nc.psum_tensor("p_ot", [128, 512], F32) as p_ot,
            nc.psum_tensor("p_kd", [128, 512], F32) as p_kd,
            nc.psum_tensor("p_t2", [128, 128], BF16) as p_t2,
            nc.psum_tensor("p_t3", [128, 128], BF16) as p_t3,
        ):
            for cix in range(NCH):
                base_ld = st.ld.v
                pe_now_ld, ve_now_ld = st.pe.v, st.ve.v
                sr_w = (st.sr[0].v, st.sr[1].v)
                with nc.Block(f"f4_ld_{cix}") as block:
                    @block.sync
                    def _(sync):
                        sync.wait_ge(st.sem_pe, pe_now_ld)
                        sync.wait_ge(st.sem_ve, ve_now_ld)
                        sync.wait_ge(st.sr[0].sem, sr_w[0])
                        sync.wait_ge(st.sr[1].sem, sr_w[1])
                        for ti in range(2):
                            row = (cix * 2 + ti) * 128
                            sync.dma_start(out=q_ch[:, ti * D:(ti + 1) * D],
                                           in_=q_d[row:row + 128, :]).then_inc(st.sem_ld, 16)
                            sync.dma_start(out=k_ch[:, ti * D:(ti + 1) * D],
                                           in_=k_d[row:row + 128, :]).then_inc(st.sem_ld, 16)
                            sync.dma_start(out=v_ch[:, ti * D:(ti + 1) * D],
                                           in_=v_d[row:row + 128, :]).then_inc(st.sem_ld, 16)
                st.ld.inc(96)
                ld_now = st.ld.v
                with nc.Block(f"f4_ldw_{cix}") as block:
                    @block.vector
                    def _(v):
                        v.wait_ge(st.sem_ld, ld_now)
                    @block.tensor
                    def _(te):
                        te.wait_ge(st.sem_ld, ld_now)
                emit_transpose_set(nc, st, f"f4_qt_{cix}",
                                   [(q_ch[:, ti * D + c * 128:ti * D + (c + 1) * 128],
                                     qf_ch[:, c * BS_LA + ti * 128:c * BS_LA + (ti + 1) * 128])
                                    for ti in range(2) for c in range(NDC)], p_t2, p_t3)
                emit_transpose_set(nc, st, f"f4_kt_{cix}",
                                   [(k_ch[:, ti * D + c * 128:ti * D + (c + 1) * 128],
                                     kf_ch[:, c * BS_LA + ti * 128:c * BS_LA + (ti + 1) * 128])
                                    for ti in range(2) for c in range(NDC)], p_t2, p_t3)
                for jt in range(2):
                    ve_war = st.ve.v
                    with nc.Block(f"f4_sT_{cix}_{jt}") as block:
                        @block.tensor
                        def _(te):
                            te.wait_ge(st.sem_ve, ve_war)
                            for c in range(NDC):
                                ins = te.matmul(p_sT[:],
                                                kf_ch[:, c * BS_LA + jt * 128:c * BS_LA + (jt + 1) * 128],
                                                qf_ch[:, c * BS_LA:(c + 1) * BS_LA],
                                                start=(c == 0), stop=(c == NDC - 1))
                                if c == NDC - 1:
                                    ins.then_inc(st.sem_pe, 1)
                    st.pe.inc()
                    pe_now = st.pe.v
                    with nc.Block(f"f4_sTe_{cix}_{jt}") as block:
                        @block.vector
                        def _(v):
                            v.wait_ge(st.sem_pe, pe_now)
                            v.tensor_tensor(out=sTm[:, jt * 256:(jt + 1) * 256], in0=p_sT[:],
                                            in1=maskA[:] if jt == 0 else maskB[:],
                                            op=ALU.mult).then_inc(st.sem_ve, 1)
                    st.ve.inc()
                sTm_ve = st.ve.v
                for it in range(2):
                    for ep in range(4):
                        ac_war = st.ac.v
                        with nc.Block(f"f4_ot_{cix}_{it}_{ep}") as block:
                            @block.tensor
                            def _(te):
                                te.wait_ge(st.sem_ve, sTm_ve)
                                te.wait_ge(st.sem_ac, ac_war)
                                for jt in range(2):
                                    te.matmul(p_ot[:],
                                              sTm[:, jt * 256 + it * 128:jt * 256 + (it + 1) * 128],
                                              v_ch[:, jt * D + ep * 512:jt * D + (ep + 1) * 512],
                                              start=(jt == 0), stop=False)
                                for c in range(NDC):
                                    ins = te.matmul(p_ot[:],
                                                    qf_ch[:, c * BS_LA + it * 128:c * BS_LA + (it + 1) * 128],
                                                    kv[:, c * D + ep * 512:c * D + (ep + 1) * 512],
                                                    start=False, stop=(c == NDC - 1))
                                    if c == NDC - 1:
                                        ins.then_inc(st.sem_pe, 1)
                        st.pe.inc()
                        pe_now = st.pe.v
                        with nc.Block(f"f4_ote_{cix}_{it}_{ep}") as block:
                            @block.scalar
                            def _(s):
                                s.wait_ge(st.sem_pe, pe_now)
                                tp2 = cix * 2 + it
                                s.activation(out=o_loc[:, tp2 * D + ep * 512:tp2 * D + (ep + 1) * 512],
                                             in_=p_ot[:], func=AF.Copy).then_inc(st.sem_ac, 1)
                        st.ac.inc()
                for dt in range(NDC):
                    for ep in range(4):
                        ve_war = st.ve.v
                        with nc.Block(f"f4_kv_{cix}_{dt}_{ep}") as block:
                            @block.tensor
                            def _(te):
                                te.wait_ge(st.sem_ve, ve_war)
                                for jt in range(2):
                                    ins = te.matmul(p_kd[:],
                                                    k_ch[:, jt * D + dt * 128:jt * D + (dt + 1) * 128],
                                                    v_ch[:, jt * D + ep * 512:jt * D + (ep + 1) * 512],
                                                    start=(jt == 0), stop=(jt == 1))
                                    if jt == 1:
                                        ins.then_inc(st.sem_pe, 1)
                        st.pe.inc()
                        pe_now = st.pe.v
                        with nc.Block(f"f4_kve_{cix}_{dt}_{ep}") as block:
                            @block.vector
                            def _(v):
                                v.wait_ge(st.sem_pe, pe_now)
                                v.tensor_tensor(out=kv[:, dt * D + ep * 512:dt * D + (ep + 1) * 512],
                                                in0=p_kd[:],
                                                in1=kv[:, dt * D + ep * 512:dt * D + (ep + 1) * 512],
                                                op=ALU.add).then_inc(st.sem_ve, 1)
                        st.ve.inc()

        ac_now, ve_now = st.ac.v, st.ve.v
        with nc.Block("f4_st") as block:
            @block.sync
            def _(sync):
                sync.wait_ge(st.sem_ac, ac_now)
                sync.wait_ge(st.sem_ve, ve_now)
                for tp in range(NT_P):
                    sync.dma_start(out=io["o_loc_d"][tp * 128:(tp + 1) * 128, :],
                                   in_=o_loc[:, tp * D:(tp + 1) * D]).then_inc(st.sem_st, 16)
                for c in range(NDC):
                    sync.dma_start(out=io["kv_out"][:, c * D:(c + 1) * D],
                                   in_=kv[:, c * D:(c + 1) * D]).then_inc(st.sem_st, 16)
        st.stc.inc(16 * (NT_P + NDC))



def emit_l1_cross(nc, ctx, st, io, no_cross=False):
    x2 = io["x2"]
    with ExitStack() as fctx:
        wp_bf = fctx.enter_context(nc.sbuf_tensor("wp_bf", [128, NDC * D], BF16))
        if not no_cross:
            kvp = fctx.enter_context(nc.sbuf_tensor("kvp", [128, NDC * D], BF16))
            parb = fctx.enter_context(nc.sbuf_tensor("parb", [128, 1], F32))
            par1 = fctx.enter_context(nc.sbuf_tensor("par1", [1, 1], F32))
            base_ld = st.ld.v
            with nc.Block("f5_ld") as block:
                @block.sync
                def _(sync):
                    sync.dma_start(out=par1[:], in_=io["par"]).then_inc(st.sem_ld, 16)
                    for c in range(NDC):
                        sync.dma_start(out=kvp[:, c * D:(c + 1) * D],
                                       in_=io["kv_prev"][:, c * D:(c + 1) * D]).then_inc(st.sem_ld, 16)
            st.ld.inc(16 * (1 + NDC))
            with nc.psum_tensor("p_b", [128, 1], F32) as p_b:
                ld_now = st.ld.v
                with nc.Block("f5_bc") as block:
                    @block.tensor
                    def _(t):
                        t.wait_ge(st.sem_ld, ld_now)
                        t.matmul(p_b[:], st.ones_f[:], par1[:], start=True,
                                 stop=True).then_inc(st.sem_pe, 1)
                st.pe.inc()
                pe_now = st.pe.v
                with nc.Block("f5_bc2") as block:
                    @block.vector
                    def _(v):
                        v.wait_ge(st.sem_pe, pe_now)
                        v.tensor_copy(out=parb[:], in_=p_b[:]).then_inc(st.sem_ve, 1)
                st.ve.inc()

        jobs = []
        for c in range(NDC):
            for dp in range(4):
                jobs.append((io["wpT"][c * 128:(c + 1) * 128, dp * 512:(dp + 1) * 512],
                             wp_bf[:, c * D + dp * 512:c * D + (dp + 1) * 512]))
        with nc.sbuf_tensor("wldst5", [128, 2 * 512], F32) as wldst5:
            emit_load_w_bf16(nc, st, "f5_ldwp", jobs, wldst5)
        wp_ve = st.ve.v

        with (
            nc.sbuf_tensor("q_t", [128, D], BF16) as q_t,
            nc.sbuf_tensor("g_t", [128, D], BF16) as g_t,
            nc.sbuf_tensor("o_t", [128, D], F32) as o_t,
            nc.sbuf_tensor("x2_t", [128, D], F32) as x2_t,
            nc.sbuf_tensor("ot_f", [128, D], F32) as ot_f,
            nc.sbuf_tensor("og_bf", [128, D], BF16) as og_bf,
            nc.sbuf_tensor("qfm_t", [128, NDC * 128], BF16) as qfm_t,
            nc.sbuf_tensor("ogfm_t", [128, NDC * 128], BF16) as ogfm_t,
            nc.sbuf_tensor("x3row", [128, D], F32) as x3row,
            nc.sbuf_tensor("lnst5", [128, 8], F32) as lnst5,
            nc.sbuf_tensor("lnsc5", [128, D], F32) as lnsc5,
            nc.psum_tensor("p_cr", [128, 512], F32) as p_cr,
            nc.psum_tensor("p_w5", [128, 512], F32) as p_w5,
            nc.psum_tensor("p_t4", [128, 128], BF16) as p_t4,
            nc.psum_tensor("p_t5", [128, 128], BF16) as p_t5,
        ):
            for tp in range(NT_P):
                base_ld = st.ld.v
                ve_w5, vs_w5 = st.ve.v, st.vs.v
                st_w5 = (st.stc.v, st.sr[0].v, st.sr[1].v)
                with nc.Block(f"f5_t_{tp}") as block:
                    @block.sync
                    def _(sync):
                        sync.wait_ge(st.sem_ve, ve_w5)
                        sync.wait_ge(st.sem_vs, vs_w5)
                        sync.wait_ge(st.sem_st, st_w5[0])
                        sync.wait_ge(st.sr[0].sem, st_w5[1])
                        sync.wait_ge(st.sr[1].sem, st_w5[2])
                        sync.dma_start(out=q_t[:], in_=io["q_tok_d"][tp * 128:(tp + 1) * 128, :]).then_inc(st.sem_ld, 16)
                        sync.dma_start(out=g_t[:], in_=io["g_tok_d"][tp * 128:(tp + 1) * 128, :]).then_inc(st.sem_ld, 16)
                        sync.dma_start(out=o_t[:], in_=io["o_loc_d"][tp * 128:(tp + 1) * 128, :]).then_inc(st.sem_ld, 16)
                        sync.dma_start(out=x2_t[:], in_=x2[tp * 128:(tp + 1) * 128, :]).then_inc(st.sem_ld, 16)
                    @block.tensor
                    def _(te):
                        te.wait_ge(st.sem_ld, base_ld + 64)
                st.ld.inc(64)
                if not no_cross:
                    emit_transpose_set(nc, st, f"f5_qt_{tp}",
                                       [(q_t[:, c * 128:(c + 1) * 128],
                                         qfm_t[:, c * 128:(c + 1) * 128]) for c in range(NDC)],
                                       p_t4, p_t5)
                    qfm_ve = st.ve.v
                    for ep in range(4):
                        vs_war = st.vs.v
                        with nc.Block(f"f5_cr_{tp}_{ep}") as block:
                            @block.tensor
                            def _(te):
                                te.wait_ge(st.sem_ve, qfm_ve)
                                te.wait_ge(st.sem_vs, vs_war)
                                for c in range(NDC):
                                    ins = te.matmul(p_cr[:], qfm_t[:, c * 128:(c + 1) * 128],
                                                    kvp[:, c * D + ep * 512:c * D + (ep + 1) * 512],
                                                    start=(c == 0), stop=(c == NDC - 1))
                                    if c == NDC - 1:
                                        ins.then_inc(st.sem_pe, 1)
                        st.pe.inc()
                        pe_now = st.pe.v
                        with nc.Block(f"f5_cre_{tp}_{ep}") as block:
                            @block.vector
                            def _(v):
                                v.wait_ge(st.sem_pe, pe_now)
                                v.scalar_tensor_tensor(out=ot_f[:, ep * 512:(ep + 1) * 512],
                                                       in0=p_cr[:], scalar=parb[:],
                                                       in1=o_t[:, ep * 512:(ep + 1) * 512],
                                                       op0=ALU.mult, op1=ALU.add).then_inc(st.sem_vs, 1)
                        st.vs.inc()
                with nc.Block(f"f5_g_{tp}") as block:
                    vs_now = st.vs.v
                    gate_src = o_t if no_cross else ot_f
                    ld_g = base_ld + 64
                    @block.vector
                    def _(v):
                        v.wait_ge(st.sem_vs, vs_now)
                        if no_cross:
                            v.wait_ge(st.sem_ld, ld_g)
                        v.tensor_tensor(out=og_bf[:], in0=gate_src[:], in1=g_t[:],
                                        op=ALU.mult).then_inc(st.sem_vs, 1)
                st.vs.inc()
                emit_transpose_set(nc, st, f"f5_ot_{tp}",
                                   [(og_bf[:, c * 128:(c + 1) * 128],
                                     ogfm_t[:, c * 128:(c + 1) * 128]) for c in range(NDC)],
                                   p_t4, p_t5, pre_wait=[(st.sem_vs, st.vs.v)])
                og_ve = st.ve.v
                for dp in range(4):
                    ve_war = st.ve.v
                    with nc.Block(f"f5_wp_{tp}_{dp}") as block:
                        @block.tensor
                        def _(te):
                            te.wait_ge(st.sem_ve, max(og_ve, wp_ve))
                            for c in range(NDC):
                                ins = te.matmul(p_w5[:], ogfm_t[:, c * 128:(c + 1) * 128],
                                                wp_bf[:, c * D + dp * 512:c * D + (dp + 1) * 512],
                                                start=(c == 0), stop=(c == NDC - 1))
                                if c == NDC - 1:
                                    ins.then_inc(st.sem_pe, 1)
                    st.pe.inc()
                    pe_now = st.pe.v
                    with nc.Block(f"f5_wpe_{tp}_{dp}") as block:
                        stc_now = st.stc.v
                        @block.vector
                        def _(v):
                            v.wait_ge(st.sem_pe, pe_now)
                            v.wait_ge(st.sem_st, stc_now)
                            v.tensor_tensor(out=x3row[:, dp * 512:(dp + 1) * 512], in0=p_w5[:],
                                            in1=x2_t[:, dp * 512:(dp + 1) * 512],
                                            op=ALU.add).then_inc(st.sem_ve, 1)
                    st.ve.inc()
                emit_ln(nc, st, f"f5_ln_{tp}", x3row[:], lnst5, lnsc5[:],
                        out_dram=io["x3_out"][tp * 128:(tp + 1) * 128, :],
                        pre_wait=[(st.sem_ve, st.ve.v)])


N_EXP = 32
I_MOE = 768
NIT = I_MOE // 128
TOP_K = 8


def emit_moe(nc, ctx, st, io):
    x3 = io["x3"]
    with ExitStack() as fctx:
        x3fm = fctx.enter_context(nc.sbuf_tensor("x3fm", [128, NDC * T_PRIM], BF16))
        macc = fctx.enter_context(nc.sbuf_tensor("macc", [128, NT_P * D], F32))
        w_n = fctx.enter_context(nc.sbuf_tensor("w_n", [128, NT_P * N_EXP], F32))
        gwT = fctx.enter_context(nc.sbuf_tensor("gwT", [128, NDC * N_EXP], F32))

        with nc.Block("f6_z") as block:
            @block.gpsimd
            def _(g):
                g.memset(macc[:], 0.0)

        base_ld = st.ld.v
        with nc.Block("f6_gw") as block:
            @block.sync
            def _(sync):
                for c in range(NDC):
                    sync.dma_start(out=gwT[:, c * N_EXP:(c + 1) * N_EXP],
                                   in_=io["gate_wT"][c * 128:(c + 1) * 128, :]).then_inc(st.sem_ld, 16)
        st.ld.inc(16 * NDC)
        gw_ld = st.ld.v

        with (
            nc.sbuf_tensor("x3t", [128, D], F32) as x3t,
            nc.sbuf_tensor("x3b", [128, D], BF16) as x3b,
            nc.sbuf_tensor("xf32", [128, NDC * 128], F32) as xf32,
            nc.sbuf_tensor("rwt", [128, N_EXP], F32) as rwt,
            nc.sbuf_tensor("cur", [128, N_EXP], F32) as cur,
            nc.sbuf_tensor("geb", [128, N_EXP], F32) as geb,
            nc.sbuf_tensor("mx6", [128, 8], F32) as mx6,
            nc.psum_tensor("p_t6", [128, 128], BF16) as p_t6,
            nc.psum_tensor("p_t7", [128, 128], BF16) as p_t7,
            nc.psum_tensor("p_tf32", [128, 128], F32) as p_tf32,
            nc.psum_tensor("p_lg", [128, N_EXP], F32) as p_lg,
        ):
            for tp in range(NT_P):
                base_ld = st.ld.v
                vs_w6, pe_w6 = st.vs.v, st.pe.v
                st_w6 = st.stc.v
                with nc.Block(f"f6_x_{tp}") as block:
                    @block.sync
                    def _(sync):
                        sync.wait_ge(st.sem_vs, vs_w6)
                        sync.wait_ge(st.sem_pe, pe_w6)
                        sync.wait_ge(st.sem_st, st_w6)
                        sync.dma_start(out=x3t[:], in_=x3[tp * 128:(tp + 1) * 128, :]).then_inc(st.sem_ld, 16)
                    @block.vector
                    def _(v):
                        v.wait_ge(st.sem_ld, base_ld + 16)
                        v.tensor_copy(out=x3b[:], in_=x3t[:]).then_inc(st.sem_vs, 1)
                st.ld.inc(16); st.vs.inc()
                emit_transpose_set(nc, st, f"f6_xt_{tp}",
                                   [(x3b[:, c * 128:(c + 1) * 128],
                                     x3fm[:, c * T_PRIM + tp * 128:c * T_PRIM + (tp + 1) * 128])
                                    for c in range(NDC)], p_t6, p_t7,
                                   pre_wait=[(st.sem_vs, st.vs.v)])
                base_pe, base_ve = st.pe.v, st.ve.v
                with nc.Block(f"f6_rt_{tp}") as block:
                    @block.tensor
                    def _(te):
                        for c in range(NDC):
                            te.wait_ge(st.sem_ve, base_ve + c)
                            te.transpose(p_tf32[:], x3t[:, c * 128:(c + 1) * 128],
                                         st.ident_f[:]).then_inc(st.sem_pe, 1)
                    @block.vector
                    def _(v):
                        for c in range(NDC):
                            v.wait_ge(st.sem_pe, base_pe + c + 1)
                            v.tensor_copy(out=xf32[:, c * 128:(c + 1) * 128],
                                          in_=p_tf32[:]).then_inc(st.sem_ve, 1)
                st.pe.inc(NDC); st.ve.inc(NDC)
                xf_ve = st.ve.v
                with nc.Block(f"f6_lg_{tp}") as block:
                    @block.tensor
                    def _(te):
                        te.wait_ge(st.sem_ve, xf_ve)
                        te.wait_ge(st.sem_ld, gw_ld)
                        for c in range(NDC):
                            ins = te.matmul(p_lg[:], xf32[:, c * 128:(c + 1) * 128],
                                            gwT[:, c * N_EXP:(c + 1) * N_EXP],
                                            start=(c == 0), stop=(c == NDC - 1))
                            if c == NDC - 1:
                                ins.then_inc(st.sem_pe, 1)
                st.pe.inc()
                pe_lg = st.pe.v
                base_ac = st.ac.v
                base_vs_sm = st.vs.v
                with nc.Block(f"f6_sm_{tp}") as block:
                    @block.vector
                    def _(v):
                        def vsync(inst):
                            inst.then_inc(st.sem_vs, 1)
                            v.wait_ge(st.sem_vs, st.vs.inc())
                        v.wait_ge(st.sem_pe, pe_lg)
                        vsync(v.reduce_max(out=mx6[:, 0:1], in_=p_lg[:], axis=AX.X, negate=True))
                        v.tensor_scalar(out=cur[:], in0=p_lg[:], scalar1=mx6[:, 0:1],
                                        scalar2=None, op0=ALU.add).then_inc(st.sem_ve, 1)
                        v.wait_ge(st.sem_ac, base_ac + 1)
                        vsync(v.reduce_sum(out=mx6[:, 1:2], in_=rwt[:], axis=AX.X))
                        vsync(v.reciprocal(out=mx6[:, 2:3], in_=mx6[:, 1:2]))
                        vsync(v.tensor_scalar(out=rwt[:], in0=rwt[:], scalar1=mx6[:, 2:3],
                                              scalar2=None, op0=ALU.mult))
                        vsync(v.tensor_copy(out=cur[:], in_=rwt[:]))
                        for it in range(TOP_K):
                            vsync(v.reduce_max(out=mx6[:, 3:4], in_=cur[:], axis=AX.X))
                            if it < TOP_K - 1:
                                vsync(v.tensor_scalar(out=geb[:], in0=cur[:], scalar1=mx6[:, 3:4],
                                                      scalar2=None, op0=ALU.is_ge))
                                vsync(v.scalar_tensor_tensor(out=cur[:], in0=geb[:], scalar=-2.0,
                                                             in1=cur[:], op0=ALU.mult, op1=ALU.add))
                        vsync(v.tensor_scalar(out=geb[:], in0=rwt[:], scalar1=mx6[:, 3:4],
                                              scalar2=None, op0=ALU.is_ge))
                        vsync(v.tensor_mul(geb[:], geb[:], rwt[:]))
                        vsync(v.reduce_sum(out=mx6[:, 5:6], in_=geb[:], axis=AX.X))
                        vsync(v.tensor_scalar(out=mx6[:, 5:6], in0=mx6[:, 5:6], scalar1=1e-9,
                                              scalar2=None, op0=ALU.max))
                        vsync(v.reciprocal(out=mx6[:, 6:7], in_=mx6[:, 5:6]))
                        v.tensor_scalar(out=w_n[:, tp * N_EXP:(tp + 1) * N_EXP],
                                        in0=geb[:], scalar1=mx6[:, 6:7], scalar2=None,
                                        op0=ALU.mult).then_inc(st.sem_vs, 1)
                    @block.scalar
                    def _(s):
                        s.wait_ge(st.sem_ve, st.ve.v + 1)
                        s.activation(out=rwt[:], in_=cur[:], func=AF.Exp).then_inc(st.sem_ac, 1)
                st.ve.inc(1); st.ac.inc(1); st.vs.inc()

        with (
            nc.sbuf_tensor("wbuf", [128, NDC * I_MOE], BF16) as wbuf,
            nc.sbuf_tensor("wdbuf", [128, NIT * D], BF16) as wdbuf,
            nc.sbuf_tensor("wldst6", [128, 2 * D], F32) as wldst6,
            nc.sbuf_tensor("gu", [128, 2 * NIT * T_PRIM], BF16) as gu,
            nc.psum_tensor("p_ga", [128, 1024], F32) as p_ga,
            nc.psum_tensor("p_gb", [128, 1024], F32) as p_gb,
            nc.psum_tensor("p_ya", [128, 1024], F32) as p_ya,
            nc.psum_tensor("p_yb", [128, 1024], F32) as p_yb,
        ):
            pe_after_g = 0
            pe_after_u = 0
            pe_after_y = 0
            for e in range(N_EXP):
                def mm_half(wv, act, half):
                    base_pe, base_ac = st.pe.v, st.ac.v
                    with nc.Block(f"f6_gh_{e}_{half}") as block:
                        @block.tensor
                        def _(te):
                            te.wait_ge(st.sem_ve, wv)
                            for itile in range(NIT):
                                pg = p_ga if itile % 2 == 0 else p_gb
                                if itile >= 2:
                                    te.wait_ge(st.sem_ac, base_ac + itile - 1)
                                else:
                                    te.wait_ge(st.sem_ac, base_ac)
                                for c in range(NDC):
                                    te.matmul(pg[:, 0:512],
                                              wbuf[:, c * I_MOE + itile * 128:c * I_MOE + (itile + 1) * 128],
                                              x3fm[:, c * T_PRIM:c * T_PRIM + 512],
                                              start=(c == 0), stop=(c == NDC - 1))
                                    i1 = te.matmul(pg[:, 512:1024],
                                                   wbuf[:, c * I_MOE + itile * 128:c * I_MOE + (itile + 1) * 128],
                                                   x3fm[:, c * T_PRIM + 512:c * T_PRIM + 1024],
                                                   start=(c == 0), stop=(c == NDC - 1))
                                    if c == NDC - 1:
                                        i1.then_inc(st.sem_pe, 1)
                        @block.scalar
                        def _(s):
                            for itile in range(NIT):
                                pg = p_ga if itile % 2 == 0 else p_gb
                                s.wait_ge(st.sem_pe, base_pe + itile + 1)
                                base_col = half * NIT * T_PRIM + itile * T_PRIM
                                s.activation(out=gu[:, base_col:base_col + 1024],
                                             in_=pg[:], func=act).then_inc(st.sem_ac, 1)
                    st.pe.inc(NIT); st.ac.inc(NIT)

                emit_load_w_bf16(nc, st, f"f6_wg_{e}",
                                 [(io["w_gate"][e, c * 128:(c + 1) * 128, :],
                                   wbuf[:, c * I_MOE:(c + 1) * I_MOE]) for c in range(NDC)],
                                 wldst6, pre_wait=[(st.sem_pe, pe_after_u)])
                mm_half(st.ve.v, AF.Silu, 0)
                pe_after_g = st.pe.v
                emit_load_w_bf16(nc, st, f"f6_wu_{e}",
                                 [(io["w_up"][e, c * 128:(c + 1) * 128, :],
                                   wbuf[:, c * I_MOE:(c + 1) * I_MOE]) for c in range(NDC)],
                                 wldst6, pre_wait=[(st.sem_pe, pe_after_g)])
                mm_half(st.ve.v, AF.Copy, 1)
                pe_after_u = st.pe.v
                ac_now = st.ac.v
                with nc.Block(f"f6_p_{e}") as block:
                    @block.gpsimd
                    def _(g):
                        g.wait_ge(st.sem_ac, ac_now)
                        g.tensor_tensor(out=gu[:, 0:NIT * T_PRIM], in0=gu[:, 0:NIT * T_PRIM],
                                        in1=gu[:, NIT * T_PRIM:2 * NIT * T_PRIM],
                                        op=ALU.mult).then_inc(st.sem_gs, 1)
                st.gs.inc()
                p_ready = st.gs.v
                emit_load_w_bf16(nc, st, f"f6_wd_{e}",
                                 [(io["w_down"][e, c * 128:(c + 1) * 128, :],
                                   wdbuf[:, c * D:(c + 1) * D]) for c in range(NIT)],
                                 wldst6, pre_wait=[(st.sem_pe, pe_after_y)])
                wd_ve = st.ve.v
                base_pe_y, base_ve_y = st.pe.v, st.ve.v
                with nc.Block(f"f6_y_{e}") as block:
                    @block.tensor
                    def _(te):
                        te.wait_ge(st.sem_ve, wd_ve)
                        te.wait_ge(st.sem_gs, p_ready)
                        for gix in range(2 * NT_P):
                            tp, dh = gix // 2, gix % 2
                            py = p_ya if gix % 2 == 0 else p_yb
                            if gix >= 2:
                                te.wait_ge(st.sem_ve, base_ve_y + 2 * (gix - 1))
                            for ic in range(NIT):
                                for dp in range(2):
                                    d0 = dh * 2 + dp
                                    ins = te.matmul(py[:, dp * 512:(dp + 1) * 512],
                                                    gu[:, ic * T_PRIM + tp * 128:ic * T_PRIM + (tp + 1) * 128],
                                                    wdbuf[:, ic * D + d0 * 512:ic * D + (d0 + 1) * 512],
                                                    start=(ic == 0), stop=(ic == NIT - 1))
                                    if ic == NIT - 1 and dp == 1:
                                        ins.then_inc(st.sem_pe, 1)
                    @block.vector
                    def _(v):
                        for gix in range(2 * NT_P):
                            tp, dh = gix // 2, gix % 2
                            py = p_ya if gix % 2 == 0 else p_yb
                            v.wait_ge(st.sem_pe, base_pe_y + gix + 1)
                            for dp in range(2):
                                d0 = dh * 2 + dp
                                v.scalar_tensor_tensor(
                                    out=macc[:, tp * D + d0 * 512:tp * D + (d0 + 1) * 512],
                                    in0=py[:, dp * 512:(dp + 1) * 512],
                                    scalar=w_n[:, tp * N_EXP + e:tp * N_EXP + e + 1],
                                    in1=macc[:, tp * D + d0 * 512:tp * D + (d0 + 1) * 512],
                                    op0=ALU.mult, op1=ALU.add).then_inc(st.sem_ve, 1)
                st.pe.inc(2 * NT_P); st.ve.inc(4 * NT_P)
                pe_after_y = st.pe.v

        with (
            nc.sbuf_tensor("x3t2", [128, D], F32) as x3t2,
            nc.sbuf_tensor("lnst6", [128, 8], F32) as lnst6,
            nc.sbuf_tensor("lnsc6", [128, D], F32) as lnsc6,
        ):
            for tp in range(NT_P):
                base_ld = st.ld.v
                vs_wf6 = st.vs.v
                st_wf6 = st.stc.v
                with nc.Block(f"f6_fin_{tp}") as block:
                    @block.sync
                    def _(sync):
                        sync.wait_ge(st.sem_vs, vs_wf6)
                        sync.wait_ge(st.sem_st, st_wf6)
                        sync.dma_start(out=x3t2[:], in_=x3[tp * 128:(tp + 1) * 128, :]).then_inc(st.sem_ld, 16)
                    @block.vector
                    def _(v):
                        v.wait_ge(st.sem_ld, base_ld + 16)
                        v.tensor_add(macc[:, tp * D:(tp + 1) * D], macc[:, tp * D:(tp + 1) * D],
                                     x3t2[:]).then_inc(st.sem_vs, 1)
                st.ld.inc(16); st.vs.inc()
                emit_ln(nc, st, f"f6_ln_{tp}", macc[:, tp * D:(tp + 1) * D], lnst6, lnsc6[:],
                        out_dram=io["out_final"][tp * 128:(tp + 1) * 128, :],
                        pre_wait=[(st.sem_vs, st.vs.v)])


def build_f3(debug=False):
    nc = bass.Bass("TRN2", target_bir_lowering=False, debug=False)
    io = {
        "x1": _dram_in(nc, "x1", [T_PRIM, D]),
        "w1T": _dram_in(nc, "w1T", [D, F_FFN]),
        "w2T": _dram_in(nc, "w2T", [F_FFN, D]),
        "b1": _dram_in(nc, "b1", [F_FFN]),
        "b2": _dram_in(nc, "b2", [D]),
        "s1": _dram_in(nc, "s1", [1, 1]),
        "s2": _dram_in(nc, "s2", [1, 1]),
        "pos0": _dram_in(nc, "pos0", [1, 1]),
        "kill": _dram_in(nc, "kill", [1, 1]),
        "qn_w": _dram_in(nc, "qn_w", [DH]),
        "kn_w": _dram_in(nc, "kn_w", [DH]),
        "scr_dram": nc.dram_tensor("scrd", [256], F32).ap(),
        "x2_out": nc.dram_tensor("x2_out", [T_PRIM, D], F32, kind="ExternalOutput").ap(),
    }
    if debug:
        io["dbg"] = {
            "r1_fm": nc.dram_tensor("d_r1fm", [128, NDC * T_PRIM], BF16, kind="ExternalOutput").ap(),
            "qsc": nc.dram_tensor("d_qsc", [128, 8], F32, kind="ExternalOutput").ap(),
            "b1q": nc.dram_tensor("d_b1q", [128, 64], F32, kind="ExternalOutput").ap(),
            "b2qb": nc.dram_tensor("d_b2qb", [128, D], F32, kind="ExternalOutput").ap(),
            "out2": nc.dram_tensor("d_out2", [128, NT_P * D], F32, kind="ExternalOutput").ap(),
        }
    st = St()
    with ExitStack() as ctx:
        emit_consts(nc, ctx, st, io)
        emit_l0_ffn(nc, ctx, st, io)
        emit_final_wait(nc, st)
    return nc


def run_f3(x1_full, inputs, debug=False):
    nc = build_f3(debug)
    maps = []
    w1T = np.ascontiguousarray(inputs["L0_ffn_W1"].T)
    w2T = np.ascontiguousarray(inputs["L0_ffn_W2"].T)
    for c in range(8):
        b, par = c // 2, c % 2
        maps.append({
            "x1": np.ascontiguousarray(x1_full[b, par * 1024:(par + 1) * 1024], np.float32),
            "w1T": w1T, "w2T": w2T,
            "b1": np.ascontiguousarray(inputs["L0_ffn_b1"]),
            "b2": np.ascontiguousarray(inputs["L0_ffn_b2"]),
            "s1": inputs["L0_ffn_s_act1"].reshape(1, 1),
            "s2": inputs["L0_ffn_s_act2"].reshape(1, 1),
            "pos0": np.array([[0.0]], np.float32),
            "kill": np.array([[0.0]], np.float32),
            "qn_w": np.ascontiguousarray(inputs["L0_attn_qn_w"]),
            "kn_w": np.ascontiguousarray(inputs["L0_attn_kn_w"]),
        })
    res = run_bass_kernel_spmd(nc, maps, core_ids=list(range(8)))
    out = np.zeros((4, 2048, 2048), np.float32)
    for c in range(8):
        b, par = c // 2, c % 2
        out[b, par * 1024:(par + 1) * 1024] = res.results[c]["x2_out"]
    if debug:
        return out, res
    return out




def build_f4(skip_scan=False):
    nc = bass.Bass("TRN2", target_bir_lowering=False, debug=False)
    io = {
        "x2": _dram_in(nc, "x2", [T_PRIM, D]),
        "l1qT": _dram_in(nc, "l1qT", [D, D]),
        "l1kT": _dram_in(nc, "l1kT", [D, D]),
        "l1vT": _dram_in(nc, "l1vT", [D, D]),
        "l1gT": _dram_in(nc, "l1gT", [D, D]),
        "pos0": _dram_in(nc, "pos0", [1, 1]),
        "kill": _dram_in(nc, "kill", [1, 1]),
        "qn_w": _dram_in(nc, "qn_w", [DH]),
        "kn_w": _dram_in(nc, "kn_w", [DH]),
        "q_tok_d": nc.dram_tensor("q_tok_d", [T_PRIM, D], BF16, kind="ExternalOutput").ap(),
        "g_tok_d": nc.dram_tensor("g_tok_d", [T_PRIM, D], BF16, kind="ExternalOutput").ap(),
        "k_tok_d": nc.dram_tensor("k_tok_d", [T_PRIM, D], BF16).ap(),
        "v_tok_d": nc.dram_tensor("v_tok_d", [T_PRIM, D], BF16).ap(),
        "o_loc_d": nc.dram_tensor("o_loc_d", [T_PRIM, D], F32, kind="ExternalOutput").ap(),
        "kv_out": nc.dram_tensor("kv_out", [128, NDC * D], BF16, kind="ExternalOutput").ap(),
    }
    st = St()
    with ExitStack() as ctx:
        emit_consts(nc, ctx, st, io)
        emit_l1_local(nc, ctx, st, io, skip_scan=skip_scan)
        emit_final_wait(nc, st)
    return nc


def build_f5(with_x3_out=True):
    nc = bass.Bass("TRN2", target_bir_lowering=False, debug=False)
    io = {
        "x2": _dram_in(nc, "x2", [T_PRIM, D]),
        "q_tok_d": nc.dram_tensor("q_tok_d", [T_PRIM, D], BF16, kind="ExternalInput").ap(),
        "g_tok_d": nc.dram_tensor("g_tok_d", [T_PRIM, D], BF16, kind="ExternalInput").ap(),
        "o_loc_d": _dram_in(nc, "o_loc_d", [T_PRIM, D]),
        "kv_prev": nc.dram_tensor("kv_prev", [128, NDC * D], BF16, kind="ExternalInput").ap(),
        "par": _dram_in(nc, "par", [1, 1]),
        "wpT": _dram_in(nc, "wpT", [D, D]),
        "pos0": _dram_in(nc, "pos0", [1, 1]),
        "kill": _dram_in(nc, "kill", [1, 1]),
        "qn_w": _dram_in(nc, "qn_w", [DH]),
        "kn_w": _dram_in(nc, "kn_w", [DH]),
        "x3_out": nc.dram_tensor("x3_out", [T_PRIM, D], F32, kind="ExternalOutput").ap(),
    }
    st = St()
    with ExitStack() as ctx:
        emit_consts(nc, ctx, st, io)
        emit_l1_cross(nc, ctx, st, io)
        emit_final_wait(nc, st)
    return nc


def _base_scalars(par):
    return {
        "pos0": np.array([[0.0]], np.float32),
        "kill": np.array([[0.0]], np.float32),
    }


def run_f45(x2_full, inputs):
    import ml_dtypes
    nc_a = build_f4()
    qT = np.ascontiguousarray(inputs["L1_attn_Wq"].T)
    kT = np.ascontiguousarray(inputs["L1_attn_Wk"].T)
    vT = np.ascontiguousarray(inputs["L1_attn_Wv"].T)
    gT = np.ascontiguousarray(inputs["L1_attn_Wg"].T)
    pT = np.ascontiguousarray(inputs["L1_attn_Wp"].T)
    qn = np.ascontiguousarray(inputs["L0_attn_qn_w"])
    kn = np.ascontiguousarray(inputs["L0_attn_kn_w"])
    maps = []
    for c in range(8):
        b, par = c // 2, c % 2
        maps.append({
            "x2": np.ascontiguousarray(x2_full[b, par * 1024:(par + 1) * 1024], np.float32),
            "l1qT": qT, "l1kT": kT, "l1vT": vT, "l1gT": gT,
            "qn_w": qn, "kn_w": kn, **_base_scalars(par),
        })
    res_a = run_bass_kernel_spmd(nc_a, maps, core_ids=list(range(8)))
    nc_b = build_f5()
    maps_b = []
    for c in range(8):
        b, par = c // 2, c % 2
        sib = c - 1 if par else c + 1
        maps_b.append({
            "x2": maps[c]["x2"],
            "q_tok_d": res_a.results[c]["q_tok_d"],
            "g_tok_d": res_a.results[c]["g_tok_d"],
            "o_loc_d": res_a.results[c]["o_loc_d"],
            "kv_prev": res_a.results[sib]["kv_out"],
            "par": np.array([[float(par)]], np.float32),
            "wpT": pT, "qn_w": qn, "kn_w": kn, **_base_scalars(par),
        })
    res_b = run_bass_kernel_spmd(nc_b, maps_b, core_ids=list(range(8)))
    out = np.zeros((4, 2048, 2048), np.float32)
    for c in range(8):
        b, par = c // 2, c % 2
        out[b, par * 1024:(par + 1) * 1024] = res_b.results[c]["x3_out"]
    return out


def build_f6():
    nc = bass.Bass("TRN2", target_bir_lowering=False, debug=False)
    io = {
        "x3": _dram_in(nc, "x3", [T_PRIM, D]),
        "gate_wT": _dram_in(nc, "gate_wT", [D, N_EXP]),
        "w_gate": _dram_in(nc, "w_gate", [N_EXP, D, I_MOE]),
        "w_up": _dram_in(nc, "w_up", [N_EXP, D, I_MOE]),
        "w_down": _dram_in(nc, "w_down", [N_EXP, I_MOE, D]),
        "pos0": _dram_in(nc, "pos0", [1, 1]),
        "kill": _dram_in(nc, "kill", [1, 1]),
        "qn_w": _dram_in(nc, "qn_w", [DH]),
        "kn_w": _dram_in(nc, "kn_w", [DH]),
        "out_final": nc.dram_tensor("out_final", [T_PRIM, D], F32, kind="ExternalOutput").ap(),
    }
    st = St()
    with ExitStack() as ctx:
        emit_consts(nc, ctx, st, io)
        emit_moe(nc, ctx, st, io)
        emit_final_wait(nc, st)
    return nc


def run_f6(x3_full, inputs):
    nc = build_f6()
    gwT = np.ascontiguousarray(inputs["L1_ffn_gate_w"].T)
    wg = np.ascontiguousarray(inputs["L1_ffn_w_gate"])
    wu = np.ascontiguousarray(inputs["L1_ffn_w_up"])
    wd = np.ascontiguousarray(inputs["L1_ffn_w_down"])
    qn = np.ascontiguousarray(inputs["L0_attn_qn_w"])
    kn = np.ascontiguousarray(inputs["L0_attn_kn_w"])
    maps = []
    for c in range(8):
        b, par = c // 2, c % 2
        maps.append({
            "x3": np.ascontiguousarray(x3_full[b, par * 1024:(par + 1) * 1024], np.float32),
            "gate_wT": gwT, "w_gate": wg, "w_up": wu, "w_down": wd,
            "qn_w": qn, "kn_w": kn, **_base_scalars(par),
        })
    res = run_bass_kernel_spmd(nc, maps, core_ids=list(range(8)))
    out = np.zeros((4, 2048, 2048), np.float32)
    for c in range(8):
        b, par = c // 2, c % 2
        out[b, par * 1024:(par + 1) * 1024] = res.results[c]["out_final"]
    return out





def build_launch_a():
    nc = bass.Bass("TRN2", target_bir_lowering=False, debug=False)
    io = {
        "xh_tok": _dram_in(nc, "xh_tok", [T_EXT, D]),
        "xh_fmT": _dram_in(nc, "xh_fmT", [D, T_EXT]),
        "wqT": _dram_in(nc, "wqT", [D, D]),
        "wkT": _dram_in(nc, "wkT", [D, 512]),
        "wvT": _dram_in(nc, "wvT", [D, 512]),
        "woT": _dram_in(nc, "woT", [D, D]),
        "qn_w": _dram_in(nc, "qn_w", [DH]),
        "kn_w": _dram_in(nc, "kn_w", [DH]),
        "pos0": _dram_in(nc, "pos0", [1, 1]),
        "kill": _dram_in(nc, "kill", [1, 1]),
        "w1T": _dram_in(nc, "w1T", [D, F_FFN]),
        "w2T": _dram_in(nc, "w2T", [F_FFN, D]),
        "b1": _dram_in(nc, "b1", [F_FFN]),
        "b2": _dram_in(nc, "b2", [D]),
        "s1": _dram_in(nc, "s1", [1, 1]),
        "s2": _dram_in(nc, "s2", [1, 1]),
        "l1qT": _dram_in(nc, "l1qT", [D, D]),
        "l1kT": _dram_in(nc, "l1kT", [D, D]),
        "l1vT": _dram_in(nc, "l1vT", [D, D]),
        "l1gT": _dram_in(nc, "l1gT", [D, D]),
        "scr_dram": nc.dram_tensor("scrd", [256], F32).ap(),
        "x1_out": nc.dram_tensor("x1_d", [T_PRIM, D], F32).ap(),
        "x2_out": nc.dram_tensor("x2_d", [T_PRIM, D], F32, kind="ExternalOutput").ap(),
        "q_tok_d": nc.dram_tensor("q_tok_d", [T_PRIM, D], BF16, kind="ExternalOutput").ap(),
        "g_tok_d": nc.dram_tensor("g_tok_d", [T_PRIM, D], BF16, kind="ExternalOutput").ap(),
        "k_tok_d": nc.dram_tensor("k_tok_d", [T_PRIM, D], BF16).ap(),
        "v_tok_d": nc.dram_tensor("v_tok_d", [T_PRIM, D], BF16).ap(),
        "o_loc_d": nc.dram_tensor("o_loc_d", [T_PRIM, D], F32, kind="ExternalOutput").ap(),
        "kv_out": nc.dram_tensor("kv_out", [128, NDC * D], BF16, kind="ExternalOutput").ap(),
    }
    st = St()
    with ExitStack() as ctx:
        emit_consts(nc, ctx, st, io)
        emit_l0_attn(nc, ctx, st, io)
        io2 = dict(io)
        io2["x1"] = io["x1_out"]
        emit_l0_ffn(nc, ctx, st, io2)
        io3 = dict(io)
        io3["x2"] = io["x2_out"]
        emit_l1_local(nc, ctx, st, io3)
        emit_final_wait(nc, st)
    return nc


def build_launch_b():
    nc = bass.Bass("TRN2", target_bir_lowering=False, debug=False)
    io = {
        "x2": _dram_in(nc, "x2", [T_PRIM, D]),
        "q_tok_d": nc.dram_tensor("q_tok_d", [T_PRIM, D], BF16, kind="ExternalInput").ap(),
        "g_tok_d": nc.dram_tensor("g_tok_d", [T_PRIM, D], BF16, kind="ExternalInput").ap(),
        "o_loc_d": _dram_in(nc, "o_loc_d", [T_PRIM, D]),
        "kv_prev": nc.dram_tensor("kv_prev", [128, NDC * D], BF16, kind="ExternalInput").ap(),
        "par": _dram_in(nc, "par", [1, 1]),
        "wpT": _dram_in(nc, "wpT", [D, D]),
        "pos0": _dram_in(nc, "pos0", [1, 1]),
        "kill": _dram_in(nc, "kill", [1, 1]),
        "qn_w": _dram_in(nc, "qn_w", [DH]),
        "kn_w": _dram_in(nc, "kn_w", [DH]),
        "gate_wT": _dram_in(nc, "gate_wT", [D, N_EXP]),
        "w_gate": _dram_in(nc, "w_gate", [N_EXP, D, I_MOE]),
        "w_up": _dram_in(nc, "w_up", [N_EXP, D, I_MOE]),
        "w_down": _dram_in(nc, "w_down", [N_EXP, I_MOE, D]),
        "x3_out": nc.dram_tensor("x3_d", [T_PRIM, D], F32).ap(),
        "out_final": nc.dram_tensor("out_final", [T_PRIM, D], F32, kind="ExternalOutput").ap(),
    }
    st = St()
    with ExitStack() as ctx:
        emit_consts(nc, ctx, st, io)
        emit_l1_cross(nc, ctx, st, io)
        io2 = dict(io)
        io2["x3"] = io["x3_out"]
        emit_moe(nc, ctx, st, io2)
        emit_final_wait(nc, st)
    return nc




def build_fused():
    nc0 = bass.Bass("TRN2", target_bir_lowering=False, debug=False)
    g = {}
    for nm, shp in [("xhA_tok", [T_EXT, D]), ("xhA_fmT", [D, T_EXT]),
                    ("xhB_tok", [T_EXT, D]), ("xhB_fmT", [D, T_EXT]),
                    ("wqT", [D, D]), ("wkT", [D, 512]), ("wvT", [D, 512]), ("woT", [D, D]),
                    ("qn_w", [DH]), ("kn_w", [DH]),
                    ("pos0A", [1, 1]), ("killA", [1, 1]),
                    ("pos0B", [1, 1]), ("killB", [1, 1]), ("par", [1, 1]),
                    ("w1T", [D, F_FFN]), ("w2T", [F_FFN, D]), ("b1", [F_FFN]), ("b2", [D]),
                    ("s1", [1, 1]), ("s2", [1, 1]),
                    ("l1qT", [D, D]), ("l1kT", [D, D]), ("l1vT", [D, D]), ("l1gT", [D, D]),
                    ("wpT", [D, D]), ("gate_wT", [D, N_EXP]),
                    ("w_gate", [N_EXP, D, I_MOE]), ("w_up", [N_EXP, D, I_MOE]),
                    ("w_down", [N_EXP, I_MOE, D])]:
        g[nm] = _dram_in(nc0, nm, shp)
    ints = {}
    for nm, shp, dt_ in [("x1A", [T_PRIM, D], F32), ("x2A", [T_PRIM, D], F32),
                         ("x1B", [T_PRIM, D], F32), ("x2B", [T_PRIM, D], F32),
                         ("qA_d", [T_PRIM, D], BF16), ("gA_d", [T_PRIM, D], BF16),
                         ("kA_d", [T_PRIM, D], BF16), ("vA_d", [T_PRIM, D], BF16),
                         ("oA_d", [T_PRIM, D], F32), ("kvA_d", [128, NDC * D], BF16),
                         ("qB_d", [T_PRIM, D], BF16), ("gB_d", [T_PRIM, D], BF16),
                         ("kB_d", [T_PRIM, D], BF16), ("vB_d", [T_PRIM, D], BF16),
                         ("oB_d", [T_PRIM, D], F32), ("kvB_d", [128, NDC * D], BF16),
                         ("x3_d", [T_PRIM, D], F32), ("scrd", [256], F32)]:
        ints[nm] = nc0.dram_tensor(nm, shp, dt_).ap()
    out_final = nc0.dram_tensor("out_final", [T_PRIM, D], F32, kind="ExternalOutput").ap()

    st = St()
    with ExitStack() as ctx:
        io_c = {"pos0": g["pos0B"], "kill": g["killB"], "qn_w": g["qn_w"], "kn_w": g["kn_w"]}
        emit_consts(nc0, ctx, st, io_c)
        for tag, xh_tok, xh_fmT, pos0, kill, x1_d, x2_d in [
                ("A", g["xhA_tok"], g["xhA_fmT"], g["pos0A"], g["killA"], ints["x1A"], ints["x2A"]),
                ("B", g["xhB_tok"], g["xhB_fmT"], g["pos0B"], g["killB"], ints["x1B"], ints["x2B"])]:
            nc = NcU(nc0, tag)
            with nc.psum_tensor("p_pk", [128, 2], F32) as p_pk:
                base_ld = st.ld.v
                with nc.Block("pk1") as block:
                    @block.sync
                    def _(sync):
                        sync.dma_start(out=st.scr11[:, 0:1], in_=pos0).then_inc(st.sem_ld, 16)
                        sync.dma_start(out=st.scr11[:, 1:2], in_=kill).then_inc(st.sem_ld, 16)
                st.ld.inc(32)
                ld_now = st.ld.v
                with nc.Block("pk2") as block:
                    @block.tensor
                    def _(t):
                        t.wait_ge(st.sem_ld, ld_now)
                        t.matmul(p_pk[:], st.ones_f[:], st.scr11[:, 0:2], start=True,
                                 stop=True).then_inc(st.sem_pe, 1)
                st.pe.inc()
                pe_now = st.pe.v
                with nc.Block("pk3") as block:
                    @block.vector
                    def _(v):
                        v.wait_ge(st.sem_pe, pe_now)
                        v.tensor_copy(out=st.pos0b[:], in_=p_pk[:, 0:1])
                        v.tensor_copy(out=st.killb[:], in_=p_pk[:, 1:2]).then_inc(st.sem_ve, 1)
                st.ve.inc()
            io1 = {"xh_tok": xh_tok, "xh_fmT": xh_fmT, "wqT": g["wqT"], "wkT": g["wkT"],
                   "wvT": g["wvT"], "woT": g["woT"], "x1_out": x1_d}
            emit_l0_attn(nc, ctx, st, io1)
            io3 = {"x1": x1_d, "w1T": g["w1T"], "w2T": g["w2T"], "b1": g["b1"],
                   "b2": g["b2"], "s1": g["s1"], "s2": g["s2"],
                   "scr_dram": ints["scrd"], "x2_out": x2_d}
            emit_l0_ffn(nc, ctx, st, io3)

        ncp = NcU(nc0, "P")
        io4p = {"x2": ints["x2A"], "l1qT": g["l1qT"], "l1kT": g["l1kT"],
                "l1vT": g["l1vT"], "l1gT": g["l1gT"],
                "q_tok_d": ints["qA_d"], "g_tok_d": ints["gA_d"],
                "k_tok_d": ints["kA_d"], "v_tok_d": ints["vA_d"],
                "o_loc_d": ints["oA_d"], "kv_out": ints["kvA_d"]}
        emit_l1_local(ncp, ctx, st, io4p, skip_scan=True)

        parb5 = ctx.enter_context(nc0.sbuf_tensor("parb5", [128, 1], F32))
        with nc0.psum_tensor("p_pb5", [128, 1], F32) as p_pb5:
            base_ld = st.ld.v
            with nc0.Block("pb51") as block:
                @block.sync
                def _(sync):
                    sync.dma_start(out=st.scr11[:, 0:1], in_=g["par"]).then_inc(st.sem_ld, 16)
            st.ld.inc(16)
            ld_now = st.ld.v
            with nc0.Block("pb52") as block:
                @block.tensor
                def _(t):
                    t.wait_ge(st.sem_ld, ld_now)
                    t.matmul(p_pb5[:], st.ones_f[:], st.scr11[:, 0:1], start=True,
                             stop=True).then_inc(st.sem_pe, 1)
            st.pe.inc()
            pe_now = st.pe.v
            with nc0.Block("pb53") as block:
                @block.vector
                def _(v):
                    v.wait_ge(st.sem_pe, pe_now)
                    v.tensor_copy(out=parb5[:], in_=p_pb5[:]).then_inc(st.sem_ve, 1)
            st.ve.inc()

        nc4 = NcU(nc0, "S")
        io4 = {"x2": ints["x2B"], "l1qT": g["l1qT"], "l1kT": g["l1kT"],
               "l1vT": g["l1vT"], "l1gT": g["l1gT"],
               "q_tok_d": ints["qB_d"], "g_tok_d": ints["gB_d"],
               "k_tok_d": ints["kB_d"], "v_tok_d": ints["vB_d"],
               "o_loc_d": ints["oB_d"], "kv_out": ints["kvB_d"]}
        emit_l1_local(nc4, ctx, st, io4,
                      prefix=(ints["kA_d"], ints["vA_d"], parb5))

        nc5 = NcU(nc0, "C")
        io5 = {"x2": ints["x2B"], "q_tok_d": ints["qB_d"], "g_tok_d": ints["gB_d"],
               "o_loc_d": ints["oB_d"], "wpT": g["wpT"], "x3_out": ints["x3_d"]}
        emit_l1_cross(nc5, ctx, st, io5, no_cross=True)
        nc6 = NcU(nc0, "M")
        io6 = {"x3": ints["x3_d"], "gate_wT": g["gate_wT"], "w_gate": g["w_gate"],
               "w_up": g["w_up"], "w_down": g["w_down"], "out_final": out_final}
        emit_moe(nc6, ctx, st, io6)
        emit_final_wait(nc0, st)
    return nc0


def kernel(**inputs):
    x = np.ascontiguousarray(inputs["L0_attn_x"], np.float32)
    wqT = np.ascontiguousarray(inputs["L0_attn_Wq"].T)
    wkT = np.ascontiguousarray(inputs["L0_attn_Wk"].T)
    wvT = np.ascontiguousarray(inputs["L0_attn_Wv"].T)
    woT = np.ascontiguousarray(inputs["L0_attn_Wo"].T)
    w1T = np.ascontiguousarray(inputs["L0_ffn_W1"].T)
    w2T = np.ascontiguousarray(inputs["L0_ffn_W2"].T)
    l1qT = np.ascontiguousarray(inputs["L1_attn_Wq"].T)
    l1kT = np.ascontiguousarray(inputs["L1_attn_Wk"].T)
    l1vT = np.ascontiguousarray(inputs["L1_attn_Wv"].T)
    l1gT = np.ascontiguousarray(inputs["L1_attn_Wg"].T)
    wpT = np.ascontiguousarray(inputs["L1_attn_Wp"].T)
    gwT = np.ascontiguousarray(inputs["L1_ffn_gate_w"].T)
    qn = np.ascontiguousarray(inputs["L0_attn_qn_w"])
    kn = np.ascontiguousarray(inputs["L0_attn_kn_w"])
    b1 = np.ascontiguousarray(inputs["L0_ffn_b1"])
    b2 = np.ascontiguousarray(inputs["L0_ffn_b2"])
    s1 = np.asarray(inputs["L0_ffn_s_act1"], np.float32).reshape(1, 1)
    s2 = np.asarray(inputs["L0_ffn_s_act2"], np.float32).reshape(1, 1)
    wg = np.ascontiguousarray(inputs["L1_ffn_w_gate"])
    wu = np.ascontiguousarray(inputs["L1_ffn_w_up"])
    wd = np.ascontiguousarray(inputs["L1_ffn_w_down"])

    nc = build_fused()
    maps = []
    for c in range(8):
        b, par = c // 2, c % 2
        xhA = np.ascontiguousarray(np.concatenate([x[b, 0:512], x[b, 0:1024]], 0))
        if par == 0:
            xhB = xhA
        else:
            xhB = np.ascontiguousarray(x[b, 512:2048])
        maps.append({
            "xhA_tok": xhA, "xhA_fmT": np.ascontiguousarray(xhA.T),
            "xhB_tok": xhB, "xhB_fmT": np.ascontiguousarray(xhB.T),
            "wqT": wqT, "wkT": wkT, "wvT": wvT, "woT": woT,
            "qn_w": qn, "kn_w": kn,
            "pos0A": np.array([[-512.0]], np.float32),
            "killA": np.array([[NEG]], np.float32),
            "pos0B": np.array([[par * 1024 - 512]], np.float32),
            "killB": np.array([[0.0 if par else NEG]], np.float32),
            "par": np.array([[float(par)]], np.float32),
            "w1T": w1T, "w2T": w2T, "b1": b1, "b2": b2, "s1": s1, "s2": s2,
            "l1qT": l1qT, "l1kT": l1kT, "l1vT": l1vT, "l1gT": l1gT,
            "wpT": wpT, "gate_wT": gwT, "w_gate": wg, "w_up": wu, "w_down": wd,
        })
    res = run_bass_kernel_spmd(nc, maps, core_ids=list(range(8)))
    out = np.zeros((4, 2048, 2048), np.float32)
    for c in range(8):
        b, par = c // 2, c % 2
        out[b, par * 1024:(par + 1) * 1024] = res.results[c]["out_final"]
    return out


def kernel_2launch(**inputs):
    x = np.ascontiguousarray(inputs["L0_attn_x"], np.float32)
    wqT = np.ascontiguousarray(inputs["L0_attn_Wq"].T)
    wkT = np.ascontiguousarray(inputs["L0_attn_Wk"].T)
    wvT = np.ascontiguousarray(inputs["L0_attn_Wv"].T)
    woT = np.ascontiguousarray(inputs["L0_attn_Wo"].T)
    w1T = np.ascontiguousarray(inputs["L0_ffn_W1"].T)
    w2T = np.ascontiguousarray(inputs["L0_ffn_W2"].T)
    l1qT = np.ascontiguousarray(inputs["L1_attn_Wq"].T)
    l1kT = np.ascontiguousarray(inputs["L1_attn_Wk"].T)
    l1vT = np.ascontiguousarray(inputs["L1_attn_Wv"].T)
    l1gT = np.ascontiguousarray(inputs["L1_attn_Wg"].T)
    wpT = np.ascontiguousarray(inputs["L1_attn_Wp"].T)
    gwT = np.ascontiguousarray(inputs["L1_ffn_gate_w"].T)
    qn = np.ascontiguousarray(inputs["L0_attn_qn_w"])
    kn = np.ascontiguousarray(inputs["L0_attn_kn_w"])
    b1 = np.ascontiguousarray(inputs["L0_ffn_b1"])
    b2 = np.ascontiguousarray(inputs["L0_ffn_b2"])
    s1 = np.asarray(inputs["L0_ffn_s_act1"], np.float32).reshape(1, 1)
    s2 = np.asarray(inputs["L0_ffn_s_act2"], np.float32).reshape(1, 1)
    wg = np.ascontiguousarray(inputs["L1_ffn_w_gate"])
    wu = np.ascontiguousarray(inputs["L1_ffn_w_up"])
    wd = np.ascontiguousarray(inputs["L1_ffn_w_down"])

    nc_a = build_launch_a()
    maps_a = []
    for c in range(8):
        b, par = c // 2, c % 2
        if par == 0:
            xh = np.concatenate([x[b, 0:512], x[b, 0:1024]], 0)
        else:
            xh = x[b, 512:2048]
        xh = np.ascontiguousarray(xh)
        maps_a.append({
            "xh_tok": xh, "xh_fmT": np.ascontiguousarray(xh.T),
            "wqT": wqT, "wkT": wkT, "wvT": wvT, "woT": woT,
            "qn_w": qn, "kn_w": kn,
            "pos0": np.array([[par * 1024 - 512]], np.float32),
            "kill": np.array([[0.0 if par else NEG]], np.float32),
            "w1T": w1T, "w2T": w2T, "b1": b1, "b2": b2, "s1": s1, "s2": s2,
            "l1qT": l1qT, "l1kT": l1kT, "l1vT": l1vT, "l1gT": l1gT,
        })
    res_a = run_bass_kernel_spmd(nc_a, maps_a, core_ids=list(range(8)))

    nc_b = build_launch_b()
    maps_b = []
    for c in range(8):
        b, par = c // 2, c % 2
        sib = c - 1 if par else c + 1
        maps_b.append({
            "x2": res_a.results[c]["x2_d"],
            "q_tok_d": res_a.results[c]["q_tok_d"],
            "g_tok_d": res_a.results[c]["g_tok_d"],
            "o_loc_d": res_a.results[c]["o_loc_d"],
            "kv_prev": res_a.results[sib]["kv_out"],
            "par": np.array([[float(par)]], np.float32),
            "wpT": wpT, "qn_w": qn, "kn_w": kn,
            "pos0": np.array([[0.0]], np.float32),
            "kill": np.array([[0.0]], np.float32),
            "gate_wT": gwT, "w_gate": wg, "w_up": wu, "w_down": wd,
        })
    res_b = run_bass_kernel_spmd(nc_b, maps_b, core_ids=list(range(8)))
    out = np.zeros((4, 2048, 2048), np.float32)
    for c in range(8):
        b, par = c // 2, c % 2
        out[b, par * 1024:(par + 1) * 1024] = res_b.results[c]["out_final"]
    return out

